# version 08
# speedup vs baseline: 1.3027x; 1.0062x over previous
import numpy as np
import ml_dtypes
import contextlib
import concourse.bass as bass
import concourse.mybir as mybir
from concourse.bass_utils import run_bass_kernel_spmd

F32 = mybir.dt.float32
BF16 = mybir.dt.bfloat16
U8 = mybir.dt.uint8
AF = mybir.ActivationFunctionType
ALU = mybir.AluOpType

NCORES = 8
B, T, DM = 4, 4096, 1024
TOWN = T // 2
QC = 64
H, HD, DST, NL, KCV = 32, 64, 128, 15, 4
DIM = H * HD
CONV = DIM + 2 * DST
PROJ = DIM + CONV + H * (NL + 1)
LN_EPS = 1e-5
RMS_EPS = 1e-5
NONE_WAIT = ()


class Sem:
    __slots__ = ("h", "n")

    def __init__(self, h):
        self.h = h
        self.n = 0


class Prog:
    def __init__(self, nc):
        self.nc = nc
        self.st = contextlib.ExitStack()
        self._sems = {}
        self.blk = None
        self._sb = {}
        self.frames = []
        self.frame_names = []

    def push_frame(self):
        self.frames.append(contextlib.ExitStack())
        self.frame_names.append([])

    def pop_frame(self):
        st = self.frames.pop()
        st.close()
        for n in self.frame_names.pop():
            self._sb.pop(n, None)

    def open_block(self):
        self.blk = self.st.enter_context(self.nc.Block())

    def sem(self, name):
        if name not in self._sems:
            self._sems[name] = Sem(self.st.enter_context(self.nc.semaphore(name)))
        return self._sems[name]

    def sb(self, name, shape, dt):
        if name not in self._sb:
            tgt = self.frames[-1] if self.frames else self.st
            self._sb[name] = tgt.enter_context(self.nc.sbuf_tensor(name, shape, dt))
            if self.frames:
                self.frame_names[-1].append(name)
        return self._sb[name]

    def psum(self, name, shape=(128, 512)):
        if name not in self._sb:
            tgt = self.frames[-1] if self.frames else self.st
            self._sb[name] = tgt.enter_context(self.nc.psum_tensor(name, list(shape), F32))
            if self.frames:
                self.frame_names[-1].append(name)
        return self._sb[name]

    def close(self):
        self.st.close()


import inspect as _inspect


def section(p, engine, sem, body, waits=()):
    emit = getattr(p.blk, engine)
    nargs = sum(1 for q in _inspect.signature(body).parameters.values()
                if q.default is _inspect.Parameter.empty)

    def f(h, body=body, waits=tuple(waits), sem=sem):
        for s, n in waits:
            if n > 0:
                h.wait_ge(s.h, n)

        def E(instr):
            sem.n += 1
            instr.then_inc(sem.h, 1)
            h.wait_ge(sem.h, sem.n)
        last = body(h) if nargs == 1 else body(h, E)
        last.then_inc(sem.h, 1)
    emit(f)
    sem.n += 1
    return (sem, sem.n)


def dma_section(p, queue, sem, pairs, waits=()):
    if not hasattr(p, 'dma_sems'):
        p.dma_sems = []
    if sem not in p.dma_sems:
        p.dma_sems.append(sem)
    emit = getattr(p.blk, queue)

    def f(h, pairs=tuple(pairs), waits=tuple(waits), sem=sem):
        for s, n in waits:
            if n > 0:
                h.wait_ge(s.h, n)
        for dst, src in pairs:
            h.dma_start(dst, src).then_inc(sem.h, 16)
    emit(f)
    sem.n += 16 * len(pairs)
    return (sem, sem.n)


def wait_only(p, engine, waits):
    emit = getattr(p.blk, engine)

    def f(h, waits=tuple(waits)):
        for s, n in waits:
            if n > 0:
                h.wait_ge(s.h, n)
    emit(f)


def tiles(view, w, n, off=0):
    return [view[:, off + i * w:off + (i + 1) * w] for i in range(n)]


def rep128(dram_ap):
    return bass.AP(dram_ap.tensor, 0, [[0, 128], [1, 1]])



def build_b2(p, xin_d, k0_d, k1_d, k2_d, s0_d, s1_d, s2_d, xout_d, town=TOWN, pfx="b2"):
    HW = 1024
    NH = town // HW
    NJ = HW // 512

    xb = p.sb(pfx + "xb", [128, 8 * HW], BF16)
    xbt = tiles(xb, HW, 8)
    hbuf = p.sb(pfx + "h", [128, 32 * HW], BF16)
    ht = tiles(hbuf, HW, 32)
    h2buf = p.sb(pfx + "h2", [128, 32 * HW], BF16)
    h2t = tiles(h2buf, HW, 32)
    scr16 = p.sb(pfx + "scr16", [128, 8192], BF16)
    wpt = tiles(scr16, 512, 8)
    ktt = tiles(scr16, 512, 8, off=4096)
    k1pan = p.sb(pfx + "k1p", [128, 2 * 32 * 256], BF16)
    k1pt = [tiles(k1pan, 256, 32, off=i * 32 * 256) for i in range(2)]
    k2pt = tiles(scr16, 256, 32)
    ones = p.sb(pfx + "ones", [128, 128], BF16)
    consts = p.sb(pfx + "c", [128, 8], F32)
    rs1 = p.sb(pfx + "rs1", [128, HW], F32)
    rs2 = p.sb(pfx + "rs2", [128, 512], F32)
    xf = k1pan[:, 0:8192].bitcast(F32)
    xft = tiles(xf, 512, 8)
    tmp3 = k1pan[:, 8192:9216].bitcast(F32)
    lnb = xb
    stat = k1pan[:, 9216:11264].bitcast(F32)
    ps = [p.psum(pfx + f"ps{i}") for i in range(8)]

    glc = p.sem(pfx + "glc")
    glx = p.sem(pfx + "glx")
    glw = p.sem(pfx + "glw")
    glk1 = [p.sem(pfx + "glk1a"), p.sem(pfx + "glk1b")]
    glk2 = p.sem(pfx + "glk2")
    hlx = p.sem(pfx + "hlx")
    hl = p.sem(pfx + "hl")
    pe = p.sem(pfx + "pe")
    dv = p.sem(pfx + "dv")
    ac = p.sem(pfx + "ac")
    gm = p.sem(pfx + "gm")

    c0 = dma_section(p, "gpsimd", glc, [(consts[:, 0:1], rep128(s0_d)),
                                       (consts[:, 1:2], rep128(s1_d)),
                                       (consts[:, 2:3], rep128(s1_d)),
                                       (consts[:, 3:4], rep128(s2_d)),
                                       (consts[:, 4:5], rep128(s2_d))])

    def cprep(v):
        v.tensor_scalar_mul(consts[:, 0:1], consts[:, 0:1], 2.0)
        v.tensor_scalar_mul(consts[:, 1:2], consts[:, 1:2], 4.0)
        v.tensor_scalar_mul(consts[:, 2:3], consts[:, 2:3], -2.0)
        v.tensor_scalar_mul(consts[:, 3:4], consts[:, 3:4], 4.0)
        return v.tensor_scalar_mul(consts[:, 4:5], consts[:, 4:5], -2.0)
    cdone = section(p, "vector", dv, cprep, waits=[c0])
    odone = section(p, "gpsimd", gm, lambda g: g.memset(ones[:, :], 1.0))

    last_store = (p.sem(pfx + "hl"), 0)
    for half in range(NH):
        th = half * HW
        xl = dma_section(p, "gpsimd", glx,
                         [(bass.AP(xb, 0, [[8 * HW, 128], [HW, 8], [1, HW]]),
                           bass.AP(xin_d.tensor, th, [[town, 128], [128 * town, 8], [1, HW]]))],
                         waits=[last_store])

        s1_mm = []
        s1_ev = []
        for pan in range(8):
            o0 = pan * 512
            wb = [s1_mm[-1]] if pan > 0 else []
            ld = dma_section(p, "gpsimd", glw,
                             [(bass.AP(scr16, 0, [[8192, 128], [512, 8], [1, 512]]),
                               bass.AP(k0_d.tensor, o0, [[4096, 128], [524288, 8], [1, 512]])),
                              (bass.AP(scr16, 4096, [[8192, 128], [512, 8], [1, 512]]),
                               bass.AP(k0_d.tensor, 1024 * 4096 + o0, [[4096, 128], [524288, 8], [1, 512]]))],
                             waits=wb)

            def sub(v):
                last = None
                for k in range(8):
                    last = v.tensor_sub(wpt[k], wpt[k], ktt[k])
                return last
            sd = section(p, "vector", dv, sub, waits=[ld])
            for oo in range(4):
                o = pan * 4 + oo
                banks = [ps[(o % 2) * NJ + j] for j in range(NJ)]
                mwaits = [sd, xl]
                if o >= 2:
                    mwaits.append(s1_ev[o - 2])
                else:
                    mwaits.append((dv, dv.n))

                def mmo(t, oo=oo, banks=banks):
                    last = None
                    for k in range(8):
                        for j, bank in enumerate(banks):
                            last = t.matmul(bank[:, :], wpt[k][:, oo * 128:(oo + 1) * 128],
                                            xbt[k][:, j * 512:(j + 1) * 512],
                                            start=(k == 0), stop=(k == 7))
                    return last
                md = section(p, "tensor", pe, mmo, waits=mwaits)
                s1_mm.append(md)

                def ev(a, o=o, banks=banks):
                    last = None
                    for j, bank in enumerate(banks):
                        last = a.activation(ht[o][:, j * 512:(j + 1) * 512], bank[:, :],
                                            AF.Relu, scale=consts[:, 0:1])
                    return last
                s1_ev.append(section(p, "scalar", ac, ev, waits=[md, cdone]))
        s1_done = s1_ev[-1]

        def mmrs(t):
            last = None
            for j in range(NJ):
                for k in range(32):
                    last = t.matmul(ps[4 + j][:, :], ones[:, :], ht[k][:, j * 512:(j + 1) * 512],
                                    start=(k == 0), stop=(k == 31))
            return last
        rsm = section(p, "tensor", pe, mmrs, waits=[s1_done, odone, (dv, dv.n)])

        def evrs(a):
            last = None
            for j in range(NJ):
                last = a.activation(rs1[:, j * 512:(j + 1) * 512], ps[4 + j][:, :],
                                    AF.Identity, scale=consts[:, 2:3])
            return last
        rs1_done = section(p, "scalar", ac, evrs, waits=[rsm])

        s2_mm = []
        s2_ev = []
        for pan in range(16):
            buf = pan % 2
            o0 = pan * 256
            wb = [s2_mm[(pan - 2) * 2 + 1]] if pan >= 2 else []
            ld = dma_section(p, "gpsimd", glk1[buf],
                             [(bass.AP(k1pan, buf * 8192, [[16384, 128], [256, 32], [1, 256]]),
                               bass.AP(k1_d.tensor, o0, [[4096, 128], [524288, 32], [1, 256]]))],
                             waits=wb)
            for oo in range(2):
                o = pan * 2 + oo
                banks = [ps[(o % 2) * NJ + j] for j in range(NJ)]
                mwaits = [ld]
                if o >= 2:
                    mwaits.append(s2_ev[o - 2])
                else:
                    mwaits.append((dv, dv.n))

                def mmo(t, buf=buf, oo=oo, banks=banks):
                    last = None
                    for k in range(32):
                        for j, bank in enumerate(banks):
                            last = t.matmul(bank[:, :], k1pt[buf][k][:, oo * 128:(oo + 1) * 128],
                                            ht[k][:, j * 512:(j + 1) * 512],
                                            start=(k == 0), stop=(k == 31))
                    return last
                md = section(p, "tensor", pe, mmo, waits=mwaits)
                s2_mm.append(md)

                def ev(v, o=o, banks=banks):
                    last = None
                    for j, bank in enumerate(banks):
                        last = v.scalar_tensor_tensor(h2t[o][:, j * 512:(j + 1) * 512], bank[:, :],
                                                      consts[:, 1:2], rs1[:, j * 512:(j + 1) * 512],
                                                      ALU.mult, ALU.add)
                    return last
                s2_ev.append(section(p, "vector", dv, ev, waits=[md, rs1_done]))

        def relu2(a):
            last = None
            for k in range(32):
                last = a.activation(h2t[k], h2t[k], AF.Relu)
            return last
        h2_done = section(p, "scalar", ac, relu2, waits=[s2_ev[-1], s2_ev[-2]])

        k2_pe_last = (pe, 0)
        for tt in range(NJ):
            t0 = tt * 512

            def mmrs2(t, t0=t0):
                last = None
                for k in range(32):
                    last = t.matmul(ps[4][:, :], ones[:, :], h2t[k][:, t0:t0 + 512],
                                    start=(k == 0), stop=(k == 31))
                return last
            rsm2 = section(p, "tensor", pe, mmrs2, waits=[h2_done, (dv, dv.n)])

            def evrs2(a):
                return a.activation(rs2[:, :], ps[4][:, :], AF.Identity, scale=consts[:, 4:5])
            rs2_done = section(p, "scalar", ac, evrs2, waits=[rsm2])

            xl2 = dma_section(p, "sync", hlx,
                              [(bass.AP(xf.tensor, xf.offset, [[8192, 128], [512, 8], [1, 512]]),
                                bass.AP(xin_d.tensor, th + t0, [[town, 128], [128 * town, 8], [1, 512]]))],
                              waits=[last_store, s2_mm[-1]])

            dbg_w = []
            if getattr(p, "dbg_xf", None) is not None and tt == 0 and half == 0:
                dbg_w = [dma_section(p, "gpsimd", p.sem("dbgxf"), [(p.dbg_xf, xft[0])], waits=[xl2])]

            def addrs2(v):
                last = None
                for o in range(8):
                    last = v.tensor_add(xft[o], xft[o], rs2[:, :])
                return last
            rs2add = section(p, "vector", dv, addrs2, waits=[xl2, rs2_done] + dbg_w)
            if getattr(p, "dbg_xf2", None) is not None and tt == 0 and half == 0:
                d2 = dma_section(p, "gpsimd", p.sem("dbgxf2"), [(p.dbg_xf2, xft[0]), (p.dbg_rs2, rs2[:, :])], waits=[rs2add])
                rs2add = (rs2add[0], rs2add[1])
                wait_only(p, "vector", [d2])

            st3 = []
            ev3s = []
            want_dbg_mm = (getattr(p, "dbg_mm", None) is not None and tt == 0 and half == 0)
            for q in range(4):
                o0 = q * 256
                wb = [st3[-1]] if q > 0 else [k2_pe_last]
                ld2 = dma_section(p, "gpsimd", glk2,
                                  [(bass.AP(scr16, 0, [[8192, 128], [256, 32], [1, 256]]),
                                    bass.AP(k2_d.tensor, o0, [[1024, 128], [131072, 32], [1, 256]]))],
                                  waits=wb)
                for oo in range(2):
                    o = q * 2 + oo
                    bank = ps[o]
                    mwaits = [ld2]
                    if q == 0 and oo == 0:
                        mwaits.append(rsm2)
                        mwaits.append((dv, dv.n))
                    if o >= 1:
                        pass

                    def mmo3(t, oo=oo, bank=bank, t0=t0):
                        last = None
                        for k in range(32):
                            last = t.matmul(bank[:, :], k2pt[k][:, oo * 128:(oo + 1) * 128],
                                            h2t[k][:, t0:t0 + 512], start=(k == 0), stop=(k == 31))
                        return last
                    if o == 4:
                        mwaits.append(rs2_done)
                    md = section(p, "tensor", pe, mmo3, waits=mwaits)
                    st3.append(md)

                    def ev3(v, o=o, bank=bank):
                        return v.scalar_tensor_tensor(xft[o], bank[:, :], consts[:, 3:4],
                                                      xft[o], ALU.mult, ALU.add)
                    ev3s.append(section(p, "vector", dv, ev3, waits=[md, rs2add]))
            k2_pe_last = st3[-1]
            if want_dbg_mm:
                def cpps(v):
                    return v.tensor_copy(p.dbg_mm_sb[:, :], ps[0][:, :])
                cpd = section(p, "vector", dv, cpps, waits=[st3[0]])
                dma_section(p, "gpsimd", p.sem("dbgmm"), [(p.dbg_mm, p.dbg_mm_sb[:, :])], waits=[cpd])

            last_store = emit_ln_fm(p, pfx, xft, xout_d, th + t0, 512, ones, ev3s[-1],
                                    [ev3s[-2]], lnb, stat, ps, pe, dv, ac, hl)


def emit_ln_fm(p, pfx, yft, xout_d, t0, W, ones, y_done, extra_waits, lnb, stat, ps,
               pe, dv, ac, hl):
    ybt = tiles(lnb, W, 8)
    sqt = tiles(lnb, W, 8, off=8 * W)

    def mk(a):
        last = None
        for i in range(8):
            a.activation(ybt[i], yft[i], AF.Identity)
            last = a.activation(sqt[i], yft[i], AF.Square)
        return last
    sq_done = section(p, "scalar", ac, mk, waits=[y_done] + list(extra_waits))
    dump_done = None
    if getattr(p, "dbg_y", None) is not None and not getattr(p, "dbg_y_used", False):
        p.dbg_y_used = True
        dump_done = dma_section(p, "gpsimd", p.sem("dbgy"), [(p.dbg_y, yft[0])], waits=[y_done])

    def mmsum(t):
        last = None
        for k in range(8):
            t.matmul(ps[4][:, 0:W], ones[:, :], ybt[k], start=(k == 0), stop=(k == 7))
        for k in range(8):
            last = t.matmul(ps[5][:, 0:W], ones[:, :], sqt[k], start=(k == 0), stop=(k == 7))
        return last
    red = section(p, "tensor", pe, mmsum, waits=[sq_done, (dv, dv.n)])

    mean = stat[:, 0:W]
    rstd = stat[:, 512:512 + W]

    def stats(v, E):
        E(v.tensor_scalar_mul(mean, ps[4][:, 0:W], 1.0 / 1024.0))
        E(v.scalar_tensor_tensor(rstd, mean, -1.0, mean, ALU.mult, ALU.mult))
        E(v.scalar_tensor_tensor(rstd, ps[5][:, 0:W], 1.0 / 1024.0, rstd, ALU.mult, ALU.add))
        return v.tensor_scalar_add(rstd, rstd, LN_EPS)
    vdone = section(p, "vector", dv, stats, waits=[red])

    def ract(a, E):
        E(a.activation(rstd, rstd, AF.Ln))
        return a.activation(rstd, rstd, AF.Exp, scale=-0.5)
    rdone = section(p, "scalar", ac, ract, waits=[vdone])

    def apply(v, E):
        last = None
        for i in range(8):
            last = v.tensor_sub(yft[i], yft[i], mean)
        E(last)
        for i in range(8):
            last = v.tensor_mul(yft[i], yft[i], rstd)
        return last
    aw = [rdone] + ([dump_done] if dump_done else [])
    adone = section(p, "vector", dv, apply, waits=aw)
    st = dma_section(p, "sync", hl,
                     [(xout_d[i * 128:(i + 1) * 128, t0:t0 + W], yft[i]) for i in range(8)],
                     waits=[adone])
    return st


def barrier(p):
    waits = [(sem, sem.n) for sem in p._sems.values() if sem.n > 0]
    for eng in ("tensor", "vector", "scalar", "gpsimd", "sync"):
        wait_only(p, eng, waits)


_TRQ = ["sync", "scalar"]


def dma_tr_section(p, sem, pairs, waits=(), qi=0):
    if not hasattr(p, 'dma_sems'):
        p.dma_sems = []
    if sem not in p.dma_sems:
        p.dma_sems.append(sem)
    emit = getattr(p.blk, _TRQ[qi % 2])

    def f(h, pairs=tuple(pairs), waits=tuple(waits), sem=sem):
        for s_, n in waits:
            if n > 0:
                h.wait_ge(s_.h, n)
        for dst, src in pairs:
            h.dma_start(dst, src, transpose=True).then_inc(sem.h, 16)
    emit(f)
    sem.n += 16 * len(pairs)
    return (sem, sem.n)


def build_wtrans(p, jobs, stg, pfx="wt"):
    t_ = [p.sem(pfx + "ta"), p.sem(pfx + "tb")]
    s_ = [p.sem(pfx + "sa"), p.sem(pfx + "sb")]
    stgt = [stg[:, 0:2048], stg[:, 2048:4096]]
    marks = {}
    i = 0
    for (nat, scrT, O, I, mark, gw) in jobs:
        for it in range(I // 128):
            for oc in range(0, O, 2048):
                w = min(2048, O - oc)
                b = i % 2
                tl = dma_tr_section(p, t_[b], [(stgt[b][:, 0:w], nat[oc:oc + w, it * 128:(it + 1) * 128])],
                                    waits=[(s_[b], s_[b].n)] + list(gw), qi=0)
                dma_section(p, "scalar", s_[b],
                            [(scrT[it * 128:(it + 1) * 128, oc:oc + w], stgt[b][:, 0:w])],
                            waits=[tl])
                i += 1
        marks[mark] = [(s_[0], s_[0].n), (s_[1], s_[1].n)]
    return marks


def ap3(handle_or_ap, offset, dims):
    return bass.AP(handle_or_ap, offset, dims)


def build_wcast(p, jobs, stg, pfx="wc", gate=()):
    stgt = [stg[:, 0:2048], stg[:, 2048:4096]]
    gc_ = [p.sem(pfx + "gca"), p.sem(pfx + "gcb")]
    ns_ = [p.sem(pfx + "nsa"), p.sem(pfx + "nsb")]
    marks = {}
    i = 0
    for (src, nat, O, I, mark) in jobs:
        for rb in range(O // 128):
            for ci in range(0, I, 2048):
                w = min(2048, I - ci)
                b = i % 2
                ld = dma_section(p, "gpsimd", gc_[b],
                                 [(stgt[b][:, 0:w], src[rb * 128:(rb + 1) * 128, ci:ci + w])],
                                 waits=[(ns_[b], ns_[b].n)] + (list(gate) if i < 2 else []))
                dma_section(p, "sync", ns_[b],
                            [(nat[rb * 128:(rb + 1) * 128, ci:ci + w], stgt[b][:, 0:w])],
                            waits=[ld])
                i += 1
        marks[mark] = [(ns_[0], ns_[0].n), (ns_[1], ns_[1].n)]
    return marks


def finish(p):
    wait_only(p, "sync", [(s, s.n) for s in getattr(p, 'dma_sems', [])])





def build_wprep(p, jobs, pfx="wp", stg_buf=None, tt_buf=None):
    stg = stg_buf if stg_buf is not None else p.sb(pfx + "stg", [128, 2 * 4096], BF16)
    stgt = [stg[:, 0:4096], stg[:, 4096:8192]]
    ttile = tt_buf if tt_buf is not None else p.sb(pfx + "tt", [128, 2 * 4864], BF16)
    tt_ = [ttile[:, 0:4864], ttile[:, 4864:2 * 4864]]
    gc_ = [p.sem(pfx + "gca"), p.sem(pfx + "gcb")]
    ns_ = [p.sem(pfx + "nsa"), p.sem(pfx + "nsb")]
    tl_ = [p.sem(pfx + "tla"), p.sem(pfx + "tlb")]
    ts_ = [p.sem(pfx + "tsa"), p.sem(pfx + "tsb")]
    nats = []
    i = 0
    for (src, scrT, O, I) in jobs:
        nat = p.nc.dram_tensor(pfx + f"nat{len(nats)}", [O, I], BF16, kind="Internal").ap()
        nats.append((nat, O, I))
        for rb in range(O // 128):
            b = i % 2
            ld = dma_section(p, "gpsimd", gc_[b],
                             [(stgt[b][:, 0:I], src[rb * 128:(rb + 1) * 128, :])],
                             waits=[(ns_[b], ns_[b].n)])
            dma_section(p, "sync", ns_[b], [(nat[rb * 128:(rb + 1) * 128, :], stgt[b][:, 0:I])],
                        waits=[ld])
            i += 1
    last = None
    j = 0
    for (src, scrT, O, I), (nat, O2, I2) in zip(jobs, nats):
        for it in range(I // 128):
            b = j % 2
            w = [(ns_[0], ns_[0].n), (ns_[1], ns_[1].n), (ts_[b], ts_[b].n)]
            tl = dma_section_tr(p, tl_[b], tt_[b][:, 0:O], nat[:, it * 128:(it + 1) * 128], waits=w)
            last = dma_section(p, "sync", ts_[b],
                               [(scrT[it * 128:(it + 1) * 128, :], tt_[b][:, 0:O])], waits=[tl])
            j += 1
    return last


def dma_section_tr(p, sem, dst, src, waits=()):
    def f(h, waits=tuple(waits), sem=sem, dst=dst, src=src):
        for s, n in waits:
            if n > 0:
                h.wait_ge(s.h, n)
        h.dma_start(dst, src, transpose=True).then_inc(sem.h, 16)
    p.blk.sync(f)
    if not hasattr(p, 'dma_sems'):
        p.dma_sems = []
    if sem not in p.dma_sems:
        p.dma_sems.append(sem)
    sem.n += 16
    return (sem, sem.n)



def build_b4(p, xin_d, wT, scales_d, xout_d, town=TOWN, pfx="b4", gate=(), gate_bc=()):
    HW = 1024
    NH = town // HW
    NJ = HW // 512

    xb = p.sb(pfx + "xb", [128, 8 * HW], BF16)
    xbt = tiles(xb, HW, 8)
    hbuf = p.sb(pfx + "h", [128, 32 * HW], BF16)
    ht = tiles(hbuf, HW, 32)
    g1 = p.sb(pfx + "g1", [128, 32 * 512], BF16)
    g1t = tiles(g1, 512, 32)
    pan = p.sb(pfx + "pan", [128, 2 * 32 * 256], BF16)
    pant = [tiles(pan, 256, 32, off=i * 32 * 256) for i in range(2)]
    panA = p.sb(pfx + "panA", [128, 2 * 8 * 512], BF16)
    panAt = [tiles(panA, 512, 8, off=i * 8 * 512) for i in range(2)]
    w3pt = tiles(panA, 256, 32)
    tmpA = p.sb(pfx + "tA", [128, HW], BF16)
    scl = p.sb(pfx + "scl", [128, 5 * 32], F32)
    sclt = tiles(scl, 32, 5)
    sclnat = p.sb(pfx + "sn", [32, 128], F32)
    xf = g1[:, 0:8192].bitcast(F32)
    xft = tiles(xf, 512, 8)
    lnb = xb
    stat = g1[:, 8192:10240].bitcast(F32)
    ones = p.sb(pfx + "ones", [128, 128], BF16)
    ps = [p.psum(pfx + f"ps{i}") for i in range(8)]

    glx = p.sem(pfx + "glx")
    glA = [p.sem(pfx + "glAa"), p.sem(pfx + "glAb")]
    glP = [p.sem(pfx + "glPa"), p.sem(pfx + "glPb")]
    glW3 = p.sem(pfx + "glW3")
    gsc = p.sem(pfx + "gsc")
    hlx = p.sem(pfx + "hlx")
    hl = p.sem(pfx + "hl")
    pe = p.sem(pfx + "pe")
    dv = p.sem(pfx + "dv")
    ac = p.sem(pfx + "ac")
    gm = p.sem(pfx + "gm")

    odone = section(p, "gpsimd", gm, lambda g: (g.memset(ones[:, :], 1.0),
                                                g.memset(sclnat[:, :], 0.0))[-1])
    sc_done = []
    for i, nm in enumerate(["w1s", "w2s", "h1s", "h2s", "w3s"]):
        n32 = 32 if nm != "w3s" else 8
        ld = dma_section(p, "gpsimd", gsc,
                         [(sclnat[0:n32, :], scales_d[nm].rearrange("(a b) -> a b", b=128))],
                         waits=[sc_done[-1] if sc_done else odone])

        def tr(v, i=i):
            last = None
            for jb in range(4):
                out = bass.AP(scl, (32 * jb) * 160 + i * 32, [[160, 32], [1, 32]])
                last = v.transpose(out, sclnat[0:32, 32 * jb:32 * jb + 32])
            return last
        sc_done.append(section(p, "vector", dv, tr, waits=[ld]))
    scales_ready = sc_done[-1]

    last_store = (hl, 0)
    for half in range(NH):
        th = half * HW
        xl = dma_section(p, "gpsimd", glx,
                         [(bass.AP(xb, 0, [[8 * HW, 128], [HW, 8], [1, HW]]),
                           bass.AP(xin_d.tensor, th, [[town, 128], [128 * town, 8], [1, HW]]))],
                         waits=[last_store])

        sA_mm = []
        sA_ev = []
        for pn in range(8):
            o0 = pn * 512
            wb = [sA_mm[-1]] if pn > 0 else (list(gate) + ([prev_sD] if half > 0 else []))
            ld = dma_section(p, "sync", glA[0],
                             [(bass.AP(panA, 0, [[8192, 128], [512, 8], [1, 512]]),
                               bass.AP(wT["w1"].tensor, o0, [[4096, 128], [524288, 8], [1, 512]])),
                              (bass.AP(panA, 4096, [[8192, 128], [512, 8], [1, 512]]),
                               bass.AP(wT["w2"].tensor, o0, [[4096, 128], [524288, 8], [1, 512]]))],
                             waits=wb)
            for oo in range(4):
                o = pn * 4 + oo
                bnkA = [ps[(o % 1) * 0 + j] for j in range(NJ)]
                bnkB = [ps[2 + j] for j in range(NJ)]
                mwaits = [ld, xl]
                if o >= 1:
                    mwaits.append(sA_ev[o - 1])
                else:
                    mwaits.append((dv, dv.n))
                    mwaits.append((ac, ac.n))

                def mmo(t, oo=oo, bnkA=bnkA, bnkB=bnkB):
                    last = None
                    for k in range(8):
                        for j in range(NJ):
                            t.matmul(bnkA[j][:, :], panAt[0][k][:, oo * 128:(oo + 1) * 128],
                                     xbt[k][:, j * 512:(j + 1) * 512], start=(k == 0), stop=(k == 7))
                            last = t.matmul(bnkB[j][:, :], panAt[1][k][:, oo * 128:(oo + 1) * 128],
                                            xbt[k][:, j * 512:(j + 1) * 512], start=(k == 0), stop=(k == 7))
                    return last
                md = section(p, "tensor", pe, mmo, waits=mwaits)
                sA_mm.append(md)

                def evA(a, o=o, bnkA=bnkA):
                    last = None
                    for j in range(NJ):
                        last = a.activation(tmpA[:, j * 512:(j + 1) * 512], bnkA[j][:, :],
                                            AF.Silu, scale=sclt[0][:, o:o + 1])
                    return last
                ad = section(p, "scalar", ac, evA, waits=[md, scales_ready])

                def evB(v, o=o, bnkB=bnkB):
                    last = None
                    for j in range(NJ):
                        last = v.scalar_tensor_tensor(ht[o][:, j * 512:(j + 1) * 512], bnkB[j][:, :],
                                                      sclt[1][:, o:o + 1], tmpA[:, j * 512:(j + 1) * 512],
                                                      ALU.mult, ALU.mult)
                    return last
                sA_ev.append(section(p, "vector", dv, evB, waits=[ad]))
        sA_done = sA_ev[-1]

        for tt in range(NJ):
            t0 = tt * 512
            sB_mm = []
            sB_ev = []
            for pn in range(16):
                buf = pn % 2
                o0 = pn * 256
                wb = [sB_mm[(pn - 2) * 2 + 1]] if pn >= 2 else ([sA_done, (ac, ac.n)] if tt == 0 else [])
                if pn < 2 and tt == 0 and half == 0:
                    wb = wb + list(gate_bc)
                ld = dma_section(p, "sync" if buf == 0 else "scalar", glP[buf],
                                 [(bass.AP(pan, buf * 8192, [[16384, 128], [256, 32], [1, 256]]),
                                   bass.AP(wT["h1"].tensor, o0, [[4096, 128], [524288, 32], [1, 256]]))],
                                 waits=wb)
                for oo in range(2):
                    o = pn * 2 + oo
                    bank = ps[4 + (o % 4)]
                    mwaits = [ld]
                    if o >= 4:
                        mwaits.append(sB_ev[o - 4])
                    else:
                        mwaits.append((dv, dv.n))
                        mwaits.append(last_store)

                    def mmo(t, buf=buf, oo=oo, bank=bank, t0=t0):
                        last = None
                        for k in range(32):
                            last = t.matmul(bank[:, :], pant[buf][k][:, oo * 128:(oo + 1) * 128],
                                            ht[k][:, t0:t0 + 512], start=(k == 0), stop=(k == 31))
                        return last
                    md = section(p, "tensor", pe, mmo, waits=mwaits)
                    sB_mm.append(md)

                    def evg(a, o=o, bank=bank):
                        return a.activation(g1t[o], bank[:, :], AF.Gelu, scale=sclt[2][:, o:o + 1])
                    sB_ev.append(section(p, "scalar", ac, evg, waits=[md]))
            sB_done = sB_ev[-1]

            sC_mm = []
            sC_ev = []
            for pn in range(16):
                buf = pn % 2
                o0 = pn * 256
                wb = [sC_mm[(pn - 2) * 2 + 1]] if pn >= 2 else [sB_mm[-1]]
                ld = dma_section(p, "sync" if buf == 0 else "scalar", glP[buf],
                                 [(bass.AP(pan, buf * 8192, [[16384, 128], [256, 32], [1, 256]]),
                                   bass.AP(wT["h2"].tensor, o0, [[4096, 128], [524288, 32], [1, 256]]))],
                                 waits=wb)
                for oo in range(2):
                    o = pn * 2 + oo
                    bank = ps[4 + (o % 4)]
                    mwaits = [ld]
                    if o >= 4:
                        mwaits.append(sC_ev[o - 4])
                    else:
                        mwaits.append((ac, ac.n))

                    def mmo(t, buf=buf, oo=oo, bank=bank):
                        last = None
                        for k in range(32):
                            last = t.matmul(bank[:, :], pant[buf][k][:, oo * 128:(oo + 1) * 128],
                                            g1t[k], start=(k == 0), stop=(k == 31))
                        return last
                    md = section(p, "tensor", pe, mmo, waits=mwaits)
                    sC_mm.append(md)

                    def evg2(a, o=o, bank=bank, t0=t0):
                        return a.activation(ht[o][:, t0:t0 + 512], bank[:, :], AF.Gelu,
                                            scale=sclt[3][:, o:o + 1])
                    sC_ev.append(section(p, "scalar", ac, evg2, waits=[md]))
            sC_done = sC_ev[-1]

            xl2 = dma_section(p, "sync", hlx,
                              [(bass.AP(g1, 0, [[16384, 128], [512, 8], [1, 512]]).bitcast(F32)
                                if False else bass.AP(xf.tensor, xf.offset, [[8192, 128], [512, 8], [1, 512]]),
                                bass.AP(xin_d.tensor, th + t0, [[town, 128], [128 * town, 8], [1, 512]]))],
                              waits=[last_store, sC_mm[-1]])
            sD = []
            evD = []
            for q in range(4):
                o0 = q * 256
                wb = [sD[-1]] if q > 0 else [(pe, pe.n)]
                ld3 = dma_section(p, "scalar", glW3,
                                  [(bass.AP(panA, 0, [[8192, 128], [256, 32], [1, 256]]),
                                    bass.AP(wT["w3"].tensor, o0, [[1024, 128], [131072, 32], [1, 256]]))],
                                  waits=wb)
                for oo in range(2):
                    o = q * 2 + oo
                    bank = ps[o]
                    mwaits = [ld3, sC_done]
                    if q == 0 and oo == 0:
                        mwaits.append((dv, dv.n))
                        mwaits.append((ac, ac.n))

                    def mmo(t, oo=oo, bank=bank, t0=t0):
                        last = None
                        for k in range(32):
                            last = t.matmul(bank[:, :], w3pt[k][:, oo * 128:(oo + 1) * 128],
                                            ht[k][:, t0:t0 + 512], start=(k == 0), stop=(k == 31))
                        return last
                    md = section(p, "tensor", pe, mmo, waits=mwaits)
                    sD.append(md)

                    def ev(v, o=o, bank=bank):
                        return v.scalar_tensor_tensor(xft[o], bank[:, :], sclt[4][:, o:o + 1],
                                                      xft[o], ALU.mult, ALU.add)
                    evD.append(section(p, "vector", dv, ev, waits=[md, xl2]))
            last_store = emit_ln_fm(p, pfx, xft, xout_d, th + t0, 512, ones, evD[-1], [],
                                    lnb, stat, ps, pe, dv, ac, hl)
            prev_sD = sD[-1]


def make_b1_program(tseq=T):
    nc = bass.Bass("TRN2", target_bir_lowering=False)
    xin = nc.dram_tensor("xin", [tseq, 1024], F32, kind="ExternalInput").ap()
    wi = nc.dram_tensor("w_in", [PROJ, 1024], F32, kind="ExternalInput").ap()
    wo = nc.dram_tensor("w_out", [1024, 2048], F32, kind="ExternalInput").ap()
    bs = {}
    for nm, shp in [("in_proj_b", [PROJ]), ("conv_w", [2304, 4]), ("dt_bias", [32]),
                    ("A_log", [32]), ("L", [32, 15]), ("D", [32]), ("norm_w", [2048]),
                    ("out_proj_b", [1024])]:
        bs[nm] = nc.dram_tensor(nm, shp, F32, kind="ExternalInput").ap()
    sbm = nc.dram_tensor("sbmask", [16], F32, kind="ExternalInput").ap()
    hm = nc.dram_tensor("hmask", [1], F32, kind="ExternalInput").ap()
    wiT = nc.dram_tensor("wiT", [1024, PROJ], BF16, kind="Internal").ap()
    woT = nc.dram_tensor("woT", [2048, 1024], BF16, kind="Internal").ap()
    xout = nc.dram_tensor("xout", [DM, tseq // 2], F32, kind="ExternalOutput").ap()
    p = Prog(nc)
    p.open_block()
    wiN = nc.dram_tensor("wiN", [PROJ, 1024], BF16, kind="Internal").ap()
    woN = nc.dram_tensor("woN", [1024, 2048], BF16, kind="Internal").ap()
    ysb_pre = p.sb("b1ysbB", [128, 4 * 2048], BF16)
    m = build_wcast(p, [(wi, wiN, PROJ, 1024, "wi"), (wo, woN, 1024, 2048, "wo")],
                    ysb_pre[:, 0:4096])
    t = build_wtrans(p, [(wiN, wiT, PROJ, 1024, "wi", m["wo"]),
                         (woN, woT, 1024, 2048, "wo", m["wo"])], ysb_pre[:, 0:4096])
    build_b1(p, xin, wiT, woT, bs, xout, sbm, hm, tseq=tseq, gate=t["wo"])
    finish(p)
    p.close()
    return nc


def make_b3_program(town=TOWN):
    nc = bass.Bass("TRN2", target_bir_lowering=False)
    xin = nc.dram_tensor("xin", [DM, town], F32, kind="ExternalInput").ap()
    wq = nc.dram_tensor("wqkv", [3072, 1024], F32, kind="ExternalInput").ap()
    wp = nc.dram_tensor("wproj", [1024, 1024], F32, kind="ExternalInput").ap()
    bp = nc.dram_tensor("bproj", [1024], F32, kind="ExternalInput").ap()
    qkvN = nc.dram_tensor("qkvN", [3072, 1024], BF16, kind="Internal").ap()
    projN = nc.dram_tensor("projN", [1024, 1024], BF16, kind="Internal").ap()
    qkvT = nc.dram_tensor("qkvT", [1024, 3072], BF16, kind="Internal").ap()
    projT = nc.dram_tensor("projT", [1024, 1024], BF16, kind="Internal").ap()
    xout = nc.dram_tensor("xout", [DM, town], F32, kind="ExternalOutput").ap()
    p = Prog(nc)
    p.open_block()
    ktm_pre = p.sb("b3ktm", [128, 16 * 1024], BF16)
    m = build_wcast(p, [(wq, qkvN, 3072, 1024, "wq"), (wp, projN, 1024, 1024, "wp")],
                    ktm_pre[:, 0:4096])
    t = build_wtrans(p, [(qkvN, qkvT, 3072, 1024, "wq", m["wp"]),
                         (projN, projT, 1024, 1024, "wp", m["wp"])], ktm_pre[:, 4096:8192])
    build_b3(p, 0, xin, qkvT, projT, bp, xout, town=town, gate=t["wp"])
    finish(p)
    p.close()
    return nc


def make_b4_program(town=TOWN):
    nc = bass.Bass("TRN2", target_bir_lowering=False)
    xin = nc.dram_tensor("xin", [DM, town], F32, kind="ExternalInput").ap()
    srcs = {}
    for nm, shp in [("w1", [4096, 1024]), ("w2", [4096, 1024]), ("h1", [4096, 4096]),
                    ("h2", [4096, 4096]), ("w3", [1024, 4096])]:
        srcs[nm] = nc.dram_tensor(nm, shp, F32, kind="ExternalInput").ap()
    scales = {}
    for nm, n in [("w1s", 4096), ("w2s", 4096), ("h1s", 4096), ("h2s", 4096), ("w3s", 1024)]:
        scales[nm] = nc.dram_tensor(nm, [n], F32, kind="ExternalInput").ap()
    wN = {}
    wT = {}
    for nm, shp in [("w1", [4096, 1024]), ("w2", [4096, 1024]), ("h1", [4096, 4096]),
                    ("h2", [4096, 4096]), ("w3", [1024, 4096])]:
        wN[nm] = nc.dram_tensor(nm + "N", shp, BF16, kind="Internal").ap()
        wT[nm] = nc.dram_tensor(nm + "T", [shp[1], shp[0]], BF16, kind="Internal").ap()
    xout = nc.dram_tensor("xout", [DM, town], F32, kind="ExternalOutput").ap()
    p = Prog(nc)
    p.open_block()
    stg4 = p.sb("wc4stg", [128, 4096], BF16)
    jobs = [(srcs[nm], wN[nm], srcs[nm].shape[0], srcs[nm].shape[1], nm)
            for nm in ["w1", "w2", "w3", "h1", "h2"]]
    m = build_wcast(p, jobs, stg4)
    t = build_wtrans(p, [(wN[nm], wT[nm], srcs[nm].shape[0], srcs[nm].shape[1], nm, m["h2"])
                         for nm in ["w1", "w2", "w3", "h1", "h2"]], stg4)
    build_b4(p, xin, wT, scales, xout, town=town, gate=t["w3"], gate_bc=t["h2"])
    finish(p)
    p.close()
    return nc





def build_b3(p, core_id, xin_d, qkvT_d, projT_d, bproj_d, xout_d, town=TOWN, pfx="b3", gate=()):
    NT = town // 128
    NJ = town // 512

    xb = p.sb(pfx + "xb", [128, 8 * town], BF16)
    xbt = tiles(xb, town, 8)
    qf = p.sb(pfx + "qf", [128, 8 * town], BF16)
    qft = tiles(qf, town, 8)
    kv = p.sb(pfx + "kv", [128, 8 * 2048], BF16)
    kvt = tiles(kv, 2048, 8)
    ktm = p.sb(pfx + "ktm", [128, 16 * 1024], BF16)
    ktmt = [ktm[:, i * 1024:(i + 1) * 1024] for i in range(NT)]
    vtm = p.sb(pfx + "vtm", [128, 16 * 1024], BF16)
    vtmt = [vtm[:, i * 1024:(i + 1) * 1024] for i in range(NT)]
    wpan = p.sb(pfx + "wp", [128, 2 * 8 * 512], BF16)
    wpt = [tiles(wpan, 512, 8, off=i * 8 * 512) for i in range(2)]
    vksb = p.sb(pfx + "vk", [33, 1024], F32)
    bd = p.sb(pfx + "bd", [128, 1024], BF16)
    dd = p.sb(pfx + "dd", [128, 32], BF16)
    dencol = p.sb(pfx + "dc", [128, 8], F32)
    bcol = p.sb(pfx + "bc", [128, 8], F32)
    bnat = p.sb(pfx + "bn", [8, 128], F32)
    rtm = p.sb(pfx + "rtm", [128, 16 * 32], F32)
    rtmt = [rtm[:, i * 32:(i + 1) * 32] for i in range(NT)]
    idn = p.sb(pfx + "idn", [128, 128], BF16)
    idf = p.sb(pfx + "idf", [128, 128], F32)
    i1 = p.sb(pfx + "i1", [128, 128], F32)
    xf = kv[:, 0:8192].bitcast(F32)
    xft = tiles(xf, 512, 8)
    stat = kv[:, 8192:10240].bitcast(F32)
    lnb = ktm[:, 0:8192]
    ones = p.sb(pfx + "ones", [128, 128], BF16)
    ps = [p.psum(pfx + f"ps{i}") for i in range(8)]
    pstb = ps[6][:, 0:64].bitcast(BF16)
    pstf = ps[6][:, 64:192]

    ibuf = p.nc.dram_tensor(pfx + "ib", [33, 1024], F32, kind="Internal").ap()
    obuf = p.nc.dram_tensor(pfx + "ob", [33, 1024], F32, kind="Internal").ap()

    glx = p.sem(pfx + "glx")
    glw = [p.sem(pfx + "glwa"), p.sem(pfx + "glwb")]
    glkv = p.sem(pfx + "glkv")
    glb = p.sem(pfx + "glb")
    gcc = p.sem(pfx + "gcc")
    cc = p.sem(pfx + "cc")
    hlx = p.sem(pfx + "hlx")
    hl = p.sem(pfx + "hl")
    pe = p.sem(pfx + "pe")
    dv = p.sem(pfx + "dv")
    ac = p.sem(pfx + "ac")
    gm = p.sem(pfx + "gm")

    def prep0(g):
        g.memset(ones[:, :], 1.0)
        g.iota(i1[:, :], [[1, 128]], channel_multiplier=0, allow_small_or_imprecise_dtypes=True)
        g.memset(bd[:, :], 0.0)
        g.memset(dd[:, :], 0.0)
        return g.iota(idf[:, :], [[0, 128]], channel_multiplier=1, allow_small_or_imprecise_dtypes=True)
    pr0 = section(p, "gpsimd", gm, prep0)

    def mkid(v, E):
        E(v.tensor_tensor(idf[:, :], i1[:, :], idf[:, :], ALU.is_equal))
        return v.tensor_copy(idn[:, :], idf[:, :])
    iddone = section(p, "vector", dv, mkid, waits=[pr0])

    bl = dma_section(p, "gpsimd", glb, [(bnat[:, :], bproj_d.rearrange("(a b) -> a b", b=128))],
                     waits=list(gate))

    def btr(t):
        return t.transpose(pstf[:, 0:8], bnat[:, :], idf[0:8, 0:8])
    btd = section(p, "tensor", pe, btr, waits=[bl, iddone])

    def bev(v):
        return v.tensor_copy(bcol[:, :], pstf[:, 0:8])
    bcd = section(p, "vector", dv, bev, waits=[btd])

    xl = dma_section(p, "gpsimd", glx,
                     [(bass.AP(xb, 0, [[8 * town, 128], [town, 8], [1, town]]),
                       bass.AP(xin_d.tensor, 0, [[town, 128], [128 * town, 8], [1, town]]))])
    kvl = dma_section(p, "scalar", glkv,
                      [(bass.AP(kv, 0, [[16384, 128], [2048, 8], [1, 2048]]),
                        bass.AP(qkvT_d.tensor, 1024, [[3072, 128], [128 * 3072, 8], [1, 2048]]))],
                      waits=list(gate))

    q_mm = []
    q_ev = []
    for pn in range(2):
        o0 = pn * 512
        wb = [q_mm[-1]] if pn > 0 else list(gate)
        ld = dma_section(p, "sync", glw[0],
                         [(bass.AP(wpan, 0, [[8192, 128], [512, 8], [1, 512]]),
                           bass.AP(qkvT_d.tensor, o0, [[3072, 128], [128 * 3072, 8], [1, 512]]))],
                         waits=wb)
        for oo in range(4):
            o = pn * 4 + oo
            banks = [ps[(o % 2) * NJ + j] for j in range(NJ)]
            mw = [ld, xl]
            if o >= 2:
                mw.append(q_ev[o - 2])
            else:
                mw.append((dv, dv.n))

            def mmo(t, oo=oo, banks=banks):
                last = None
                for k in range(8):
                    for j, b in enumerate(banks):
                        last = t.matmul(b[:, :], wpt[0][k][:, oo * 128:(oo + 1) * 128],
                                        xbt[k][:, j * 512:(j + 1) * 512], start=(k == 0), stop=(k == 7))
                return last
            md = section(p, "tensor", pe, mmo, waits=mw)
            q_mm.append(md)

            def ev(a, o=o, banks=banks):
                last = None
                for j, b in enumerate(banks):
                    last = a.activation(qft[o][:, j * 512:(j + 1) * 512], b[:, :], AF.Relu)
                return last
            q_ev.append(section(p, "scalar", ac, ev, waits=[md]))

    kv_mm = []
    kv_ev = []
    for tt in range(NT):
        t0 = tt * 128
        banks = [ps[(tt % 2) * 4 + j] for j in range(4)]
        mw = [kvl, xl]
        if tt >= 2:
            mw.append(kv_ev[2 * (tt - 2) + 1])
        else:
            mw.append(q_ev[-1])
            mw.append((dv, dv.n))

        def mmo(t, t0=t0, banks=banks):
            last = None
            for k in range(8):
                for j, b in enumerate(banks):
                    last = t.matmul(b[:, :], xbt[k][:, t0:t0 + 128],
                                    kvt[k][:, j * 512:(j + 1) * 512], start=(k == 0), stop=(k == 7))
            return last
        md = section(p, "tensor", pe, mmo, waits=mw)
        kv_mm.append(md)

        def evk(a, tt=tt, banks=banks):
            a.activation(ktmt[tt][:, 0:512], banks[0][:, :], AF.Relu)
            return a.activation(ktmt[tt][:, 512:1024], banks[1][:, :], AF.Relu)
        kv_ev.append(section(p, "scalar", ac, evk, waits=[md]))

        def evv(v, tt=tt, banks=banks):
            v.tensor_copy(vtmt[tt][:, 0:512], banks[2][:, :])
            return v.tensor_copy(vtmt[tt][:, 512:1024], banks[3][:, :])
        kv_ev.append(section(p, "vector", dv, evv, waits=[md]))

    def dmm(t):
        last = None
        for j in range(2):
            for tt in range(NT):
                last = t.matmul(ps[4 + j][0:1, 0:512], ones[:, 0:1],
                                ktmt[tt][:, j * 512:(j + 1) * 512], start=(tt == 0), stop=(tt == NT - 1))
        return last
    dmd = section(p, "tensor", pe, dmm, waits=[kv_ev[-1], kv_ev[-2], (dv, dv.n), (ac, ac.n)])

    def dev(v):
        v.tensor_copy(vksb[32:33, 0:512], ps[4][0:1, 0:512])
        return v.tensor_copy(vksb[32:33, 512:1024], ps[5][0:1, 0:512])
    dend = section(p, "vector", dv, dev, waits=[dmd])

    vk_mm = []
    vk_ev = []
    for h in range(32):
        bank = ps[4 + (h % 2)]

        def mmo(t, h=h, bank=bank):
            last = None
            for tt in range(NT):
                last = t.matmul(bank[0:32, 0:32], ktmt[tt][:, 32 * h:32 * h + 32],
                                vtmt[tt][:, 32 * h:32 * h + 32], start=(tt == 0), stop=(tt == NT - 1))
            return last
        mw = [dend] if h <= 1 else [vk_ev[h - 2]]
        md = section(p, "tensor", pe, mmo, waits=mw)
        vk_mm.append(md)

        def ev(v, h=h, bank=bank):
            return v.tensor_copy(vksb[0:32, 32 * h:32 * h + 32], bank[0:32, 0:32])
        vk_ev.append(section(p, "vector", dv, ev, waits=[md]))

    st1 = dma_section(p, "gpsimd", gcc, [(ibuf[:, :], vksb[:, :])], waits=[vk_ev[-1], dend])

    def coll(g):
        return g.collective_compute("AllReduce", ALU.add,
                                    replica_groups=[[0, 1], [2, 3], [4, 5], [6, 7]],
                                    ins=[ibuf.tensor.ap().opt()], outs=[obuf.tensor.ap().opt()])
    cd = section(p, "gpsimd", cc, coll, waits=[st1])
    ld2 = dma_section(p, "gpsimd", gcc, [(vksb[:, :], obuf[:, :])], waits=[cd, (dv, dv.n)])

    def dtr(t):
        last = None
        for sseg in range(8):
            last = t.transpose(pstf[:, 8 + sseg:9 + sseg], vksb[32:33, 128 * sseg:128 * (sseg + 1)], idf[32:33, 32:33])
        return last
    dtd = section(p, "tensor", pe, dtr, waits=[ld2, btd, (dv, dv.n)])

    def dce(v):
        return v.tensor_copy(dencol[:, :], pstf[:, 8:16])
    dcd = section(p, "vector", dv, dce, waits=[dtd])

    def bdf(v, E):
        last = None
        for g in range(8):
            for j in range(4):
                h = 4 * g + j
                v.tensor_copy(bd[32 * j:32 * j + 32, 128 * g + 32 * j:128 * g + 32 * j + 32],
                              vksb[0:32, 32 * h:32 * h + 32])
                last = v.tensor_copy(dd[32 * j:32 * j + 32, 4 * g + j:4 * g + j + 1],
                                     dencol[32 * j:32 * j + 32, g:g + 1])
        return last
    bdd = section(p, "vector", dv, bdf, waits=[dcd, ld2, pr0])

    att_ev = []
    for tt in range(NT):
        t0 = tt * 128
        dbank = ps[7]

        def dmm2(t, t0=t0, dbank=dbank):
            last = None
            for g in range(8):
                last = t.matmul(dbank[0:128, 0:32], qft[g][:, t0:t0 + 128], dd[:, 4 * g:4 * g + 4]
                                if False else dd[:, :][:, 4 * g:4 * g + 4],
                                start=(g == 0), stop=(g == 7))
            return last
        del dmm2

        def dmm2(t, t0=t0, dbank=dbank):
            last = None
            for g in range(8):
                last = t.matmul(dbank[0:128, 4 * g:4 * g + 4], qft[g][:, t0:t0 + 128],
                                dd[:, 4 * g:4 * g + 4], start=(g == 0), stop=(g == 7),
                                skip_group_check=True)
            return last
        mw = [bdd] if tt == 0 else [att_ev[-1]]
        dmd2 = section(p, "tensor", pe, dmm2, waits=mw)

        def rv(v, E, tt=tt, dbank=dbank):
            E(v.tensor_scalar_add(rtmt[tt], dbank[0:128, 0:32], 1e-15))
            return v.reciprocal(rtmt[tt], rtmt[tt])
        rvd = section(p, "vector", dv, rv, waits=[dmd2])

        nbank = ps[tt % 2]

        def nmm(t, t0=t0, nbank=nbank):
            last = None
            for g in range(8):
                last = t.matmul(nbank[:, 0:128] if False else nbank[:, :][:, 0:128],
                                qft[g][:, t0:t0 + 128], bd[:, 128 * g:128 * (g + 1)],
                                start=True, stop=True)
            return last
        del nmm
        def natt(t, tt=tt, t0=t0):
            last = None
            for g in range(8):
                b = ps[g % 4]
                last = t.matmul(b[:, 0:128], qft[g][:, t0:t0 + 128], bd[:, 128 * g:128 * (g + 1)],
                                start=True, stop=True)
            return last
        for g in range(8):
            b = ps[g % 4]

            def m1(t, g=g, t0=t0, b=b):
                return t.matmul(b[:, 0:128], qft[g][:, t0:t0 + 128], bd[:, 128 * g:128 * (g + 1)],
                                start=True, stop=True)
            mw2 = [rvd] if g == 0 else []
            if tt == 0 and g == 0:
                mw2.append(bdd)
            if g >= 4:
                mw2.append(att_ev[-4])
            elif tt > 0:
                mw2.append(att_ev[-(4 - g)])
            md2 = section(p, "tensor", pe, m1, waits=mw2)

            def ev(v, g=g, tt=tt, b=b):
                rb = bass.AP(rtm, tt * 32 * 128 * 0 + 0, [[512, 128], [1, 4], [0, 32]]) if False else None
                rbv = bass.AP(rtm, tt * 32 + 4 * g, [[512, 128], [1, 4], [0, 32]])
                return v.tensor_tensor(vtmt[tt][:, 128 * g:128 * (g + 1)], b[:, 0:128], rbv, ALU.mult)
            att_ev.append(section(p, "vector", dv, ev, waits=[md2]))
    att_done = att_ev[-1]

    tr_ev = []
    for ct in range(8):
        for tt in range(NT):
            def trp(t, ct=ct, tt=tt):
                return t.transpose(pstb[:, 0:128], vtmt[tt][:, 128 * ct:128 * (ct + 1)], idn[:, :])
            mw = [att_done, (dv, dv.n)] if (ct == 0 and tt == 0) else [tr_ev[-1]]
            td = section(p, "tensor", pe, trp, waits=mw)

            def te(v, ct=ct, tt=tt):
                return v.tensor_copy(qft[ct][:, 128 * tt:128 * (tt + 1)], pstb[:, 0:128])
            tr_ev.append(section(p, "vector", dv, te, waits=[td]))
    trd = tr_ev[-1]

    last_store = (hl, 0)
    pj_mm = []
    for tt4 in range(NJ):
        t0 = tt4 * 512
        xl2 = dma_section(p, "sync", hlx,
                          [(bass.AP(xf.tensor, xf.offset, [[8192, 128], [512, 8], [1, 512]]),
                            bass.AP(xin_d.tensor, t0, [[town, 128], [128 * town, 8], [1, 512]]))],
                          waits=[last_store, (pe, pe.n)])
        wl = dma_section(p, "sync", glw[1],
                         [(bass.AP(wpan, 4096, [[8192, 128], [512, 8], [1, 512]]),
                           bass.AP(projT_d.tensor, 0, [[1024, 128], [131072, 8], [1, 512]])),
                          (bass.AP(wpan, 0, [[8192, 128], [512, 8], [1, 512]]),
                           bass.AP(projT_d.tensor, 512, [[1024, 128], [131072, 8], [1, 512]]))],
                         waits=[trd] if tt4 == 0 else [pj_mm[-1]])
        evs = []
        for o in range(8):
            bank = ps[o]
            wp = wpt[1] if o < 4 else wpt[0]
            ocol = o * 128 if o < 4 else (o - 4) * 128

            def mmo(t, o=o, bank=bank, wp=wp, ocol=ocol, t0=t0):
                last = None
                for k in range(8):
                    last = t.matmul(bank[:, :], wp[k][:, ocol:ocol + 128], qft[k][:, t0:t0 + 512],
                                    start=(k == 0), stop=(k == 7))
                return last
            mw = [wl, trd]
            if o == 0:
                mw.append((dv, dv.n))
                mw.append((ac, ac.n))
            md = section(p, "tensor", pe, mmo, waits=mw)
            pj_mm.append(md)

            def ev(v, o=o, bank=bank):
                return v.scalar_tensor_tensor(xft[o], bank[:, :], bcol[:, o:o + 1], xft[o],
                                              ALU.add, ALU.add)
            evs.append(section(p, "vector", dv, ev, waits=[md, xl2, bcd]))
        last_store = emit_ln_fm(p, pfx, xft, xout_d, t0, 512, ones, evs[-1], [],
                                lnb, stat, ps, pe, dv, ac, hl)





QCB = 128


def fenwick_sched(nch):
    sched = []
    slots = {}
    for i in range(nch):
        segs = [(k, slots[k]) for k in sorted(slots.keys(), reverse=True)]
        merges = []
        cur_k, cur_end = 0, i
        while cur_k in slots:
            merges.append((cur_k, slots[cur_k], cur_end))
            del slots[cur_k]
            cur_k += 1
        slots[cur_k] = cur_end
        sched.append({"segs": segs, "merges": merges, "slot": cur_k})
    return sched


def lvl_masks_np():
    t = np.arange(QCB)[:, None]
    sx = np.arange(QCB)[None, :]
    lvl = np.zeros((QCB, QCB), np.int32)
    for k in range(7):
        lo = t >> k + 1 << k + 1
        hi = t >> k << k
        m = ((t >> k & 1) == 1) & (sx >= lo) & (sx < hi)
        lvl = np.where(m, k + 1, lvl)
    causal = sx <= t
    return np.stack([((lvl == l) & causal) for l in range(8)], 0).astype(ml_dtypes.bfloat16)


def build_b1(p, xin_d, wTi, wTo, bs, xout_d, sbm_d, hm_d, tseq=T, pfx="b1", gate=()):
    nc = p.nc
    town = tseq // 2
    PRE = town
    TF = PRE + town
    NCF = TF // QCB
    NCH = tseq // QCB
    C0 = PRE // QCB
    NSB = TF // 512
    SB0 = PRE // 512
    sched = fenwick_sched(NCH)
    NTT = town // 128

    xbc_raw = nc.dram_tensor(pfx + "xbc", [2304, TF], BF16, kind="Internal").ap()
    zbuf = nc.dram_tensor(pfx + "zb", [town, DIM], BF16, kind="Internal").ap()
    xbf = nc.dram_tensor(pfx + "xbf", [TF, DM], BF16, kind="Internal").ap()

    masks_d = nc.inline_tensor(np.ascontiguousarray(lvl_masks_np().transpose(1, 0, 2).reshape(QCB, 8 * QCB)), name=pfx + "msk").ap()
    idn_d = nc.inline_tensor(np.eye(128, dtype=ml_dtypes.bfloat16), name=pfx + "idn").ap()
    idf_d = nc.inline_tensor(np.eye(128, dtype=np.float32), name=pfx + "idf").ap()

    masks = p.sb(pfx + "masks", [128, 8 * QCB], BF16)
    maskt = tiles(masks, QCB, 8)
    idn = p.sb(pfx + "idnS", [128, 128], BF16)
    idf = p.sb(pfx + "idfS", [128, 128], F32)
    ones = p.sb(pfx + "onesS", [128, 128], BF16)
    onesf = p.sb(pfx + "onesF", [128, 128], F32)
    a0scr = p.sb(pfx + "a0scr", [128, 16384], BF16)
    xT = a0scr[:, 0:4096]
    xTt = tiles(xT, 512, 8)
    wpan = a0scr[:, 4096:12288]
    wpt = [tiles(wpan, 512, 8, off=i * 8 * 512) for i in range(2)]
    wdt = a0scr[:, 12288:16384]
    wdtt = tiles(wdt, 512, 8)
    wop = a0scr
    wopt = tiles(wop, 1024, 16)
    zrep = p.sb(pfx + "zrep", [128, 2048], F32)
    nwrep = zrep
    drep = p.sb(pfx + "drep", [128, 512], F32)
    bxbc = p.sb(pfx + "bxbc", [128, 18], F32)
    bop = p.sb(pfx + "bop", [128, 8], F32)
    bnat = p.sb(pfx + "bnat", [32, 256], F32)
    convw = p.sb(pfx + "convw", [128, 18 * 4], F32)
    smcol = p.sb(pfx + "smc", [128, 16], F32)
    hmc = p.sb(pfx + "hmc", [128, 1], F32)
    x_tm = p.sb(pfx + "xtm", [128, 4 * 2048], BF16)
    x_tmt = tiles(x_tm, 2048, 4)
    z_sb = p.sb(pfx + "zsb", [128, 4 * 2048], BF16)
    z_sbt = tiles(z_sb, 2048, 4)
    y_sb = p.sb(pfx + "ysbB", [128, 4 * 2048], BF16)
    y_sbt = tiles(y_sb, 2048, 4)
    ypT = z_sb
    ypTt = tiles(ypT, 512, 16)
    plA = x_tm.bitcast(F32)
    plB = z_sb.bitcast(F32)
    plL = p.sb(pfx + "plL", [128, 4096], F32)
    dtf = plA[0:32, 0:4096]
    gpl = plA[32:64, 0:4096]
    spt = plB[0:32, 0:4096]
    zerop = plB[32:64, 0:4096]
    dbef = plB[64:96, 0:4096]
    gcf = plL[0:32, 0:4096]
    uf = plL[32:64, 0:4096]
    uflat = plL[64:65, 0:4096]
    negA = p.sb(pfx + "negA", [32, 1], F32)
    dtb = p.sb(pfx + "dtbS", [32, 1], F32)
    g1all = p.sb(pfx + "g1a", [32, 2 * 32], F32)
    emt = p.sb(pfx + "emt", [32, 32], F32)
    dt_tm = p.sb(pfx + "dttm", [128, 32 * 32], F32)
    u_tm = p.sb(pfx + "utm", [128, 32 * 32], F32)
    dbe_tm = p.sb(pfx + "dbetm", [128, 32 * 32], F32)
    dlbuf = nc.dram_tensor(pfx + "dl", [TOWN, 480], BF16, kind="Internal").ap()
    dls = p.sb(pfx + "dls", [128, 4 * 480], BF16)
    lam_tm = p.sb(pfx + "lamtm", [128, 4 * 480], F32)
    lrep = p.sb(pfx + "lrep", [128, 480], F32)
    dcols = p.sb(pfx + "dcols", [128, 32], F32)
    emrep = p.sb(pfx + "emrep", [128, 1024], F32)
    rho = p.sb(pfx + "rho", [128, 6 * 32], F32)
    scT = p.sb(pfx + "scT", [128, 6 * 32], F32)
    scsc = bnat[0:32, 0:128]
    xbc_c = p.sb(pfx + "xbcc", [128, 6 * 516], BF16)
    xbc_ct = tiles(xbc_c, 516, 6)
    sqs_scr = xbc_c[:, 0:2048]
    xbc_o = p.sb(pfx + "xbco", [128, 18 * 512], BF16)
    xbc_ot = tiles(xbc_o, 512, 18)
    lnb = xbc_o[:, 0:8192]
    stackb = p.sb(pfx + "stk", [128, 32 * 5 * 64], BF16)

    def slot_ap(h, k):
        assert k < 5
        return stackb[:, (h * 5 + k) * 64:(h * 5 + k) * 64 + 64]
    evb_z = p.sb(pfx + "evbz", [128, 1024], F32)
    zsil = y_sb[:, 0:4096]
    qk_sb = p.sb(pfx + "qks", [128, 128], BF16)
    bmt_sb = p.sb(pfx + "bmt", [128, 128], BF16)
    dti = p.sb(pfx + "dti", [128, 2 * 128], F32)
    dec = p.sb(pfx + "dec", [128, 2 * 128], BF16)
    ltl = p.sb(pfx + "ltl", [128, 2 * 128], F32)
    attb = p.sb(pfx + "attb", [128, 2 * 128], BF16)
    atT = p.sb(pfx + "atT", [128, 4 * 128], BF16)
    vp2 = p.sb(pfx + "vp2", [128, 4 * 128], BF16)
    yh = p.sb(pfx + "yh", [128, 4 * 64], F32)
    cta = p.sb(pfx + "cta", [128, 512], BF16)
    ctb = p.sb(pfx + "ctbS", [128, 512], BF16)
    ctc = p.sb(pfx + "ctcS", [128, 512], BF16)
    rmsc = p.sb(pfx + "rms", [128, 8], F32)
    xfB = x_tm.bitcast(F32)
    xftB = tiles(xfB, 512, 8)
    statB = evb_z
    ps = [p.psum(pfx + f"ps{i}") for i in range(8)]
    pstb = ps[6][:, 0:64].bitcast(BF16)
    pst2 = [ps[6][:, 0:64].bitcast(BF16), ps[6][:, 64:128].bitcast(BF16)]
    pstf = ps[6][:, 64:192]

    glc = p.sem(pfx + "glc")
    glx = p.sem(pfx + "glx")
    glw = [p.sem(pfx + "glwa"), p.sem(pfx + "glwb")]
    glxb = p.sem(pfx + "glxb")
    glxc = p.sem(pfx + "glxc")
    glz = p.sem(pfx + "glz")
    gldl = p.sem(pfx + "gldl")
    glu = p.sem(pfx + "glu")
    glo = p.sem(pfx + "glo")
    hlx = p.sem(pfx + "hlx")
    hlt = p.sem(pfx + "hlt")
    hl = p.sem(pfx + "hl")
    pe = p.sem(pfx + "pe")
    dv = p.sem(pfx + "dv")
    ac = p.sem(pfx + "ac")
    po = p.sem(pfx + "po")
    gm = p.sem(pfx + "gm")

    cl = dma_section(p, "gpsimd", glc, [
        (masks[:, :], masks_d[:, :]),
        (idn[:, :], idn_d[:, :]),
        (idf[:, :], idf_d[:, :]),
        (convw[:, :], bass.AP(bs["conv_w"].tensor, 0, [[4, 128], [512, 18], [1, 4]])),
        (zrep[:, :], bass.AP(bs["in_proj_b"].tensor, 0, [[0, 128], [1, 2048]])),
        (drep[:, :], bass.AP(bs["in_proj_b"].tensor, 4352, [[0, 128], [1, 512]])),
        (lrep[:, :], bass.AP(bs["L"].tensor, 0, [[0, 128], [1, 480]])),
        (dcols[:, :], bass.AP(bs["D"].tensor, 0, [[0, 128], [1, 32]])),
        (negA[:, :], bs["A_log"].rearrange("(a b) -> a b", b=1)),
        (dtb[:, :], bs["dt_bias"].rearrange("(a b) -> a b", b=1)),
        (bnat[0:18, 0:128], bass.AP(bs["in_proj_b"].tensor, 2048, [[128, 18], [1, 128]])),
        (bnat[0:8, 128:256], bass.AP(bs["out_proj_b"].tensor, 0, [[128, 8], [1, 128]])),
        (smcol[:, 0:NSB], bass.AP(sbm_d.tensor, 0, [[0, 128], [1, NSB]])),
        (hmc[:, :], bass.AP(hm_d.tensor, 0, [[0, 128], [1, 1]])),
    ], waits=list(gate))
    odone = section(p, "gpsimd", gm, lambda g: (g.memset(ones[:, :], 1.0),
                                                g.memset(onesf[:, :], 1.0))[-1])

    def prepc(t):
        t.transpose(pstf[:, 0:18], bnat[0:18, 0:128], idf[0:18, 0:18])
        return t.transpose(pstf[:, 18:26], bnat[0:8, 128:256], idf[0:8, 0:8])
    trc = section(p, "tensor", pe, prepc, waits=[cl])

    def evc(v, E):
        E(v.tensor_copy(bxbc[:, :], pstf[:, 0:18]))
        return v.tensor_copy(bop[:, :], pstf[:, 18:26])
    evc1 = section(p, "vector", dv, evc, waits=[trc])

    def expA(a):
        return a.activation(negA[:, :], negA[:, :], AF.Exp)
    eA = section(p, "scalar", ac, expA, waits=[cl])

    def negAv(v):
        return v.tensor_scalar_mul(negA[:, :], negA[:, :], -1.0)
    eA2 = section(p, "vector", dv, negAv, waits=[eA])

    glwd = p.sem(pfx + "glwd")
    wdl = dma_section(p, "sync", glwd,
                      [(bass.AP(a0scr, 12288, [[16384, 128], [512, 8], [1, 512]]),
                        bass.AP(wTi.tensor, 4352, [[4864, 128], [128 * 4864, 8], [1, 512]]))],
                      waits=list(gate))

    xT_scr = nc.dram_tensor(pfx + "xts", [DM, TF], BF16, kind="Internal").ap()
    xcast = []
    for rb in range(TF // 512):
        ld = dma_section(p, "gpsimd", glx,
                         [(bass.AP(a0scr, 0, [[16384, 128], [1024, 4], [512, 2], [1, 512]]),
                           bass.AP(xin_d.tensor, rb * 512 * 1024,
                                   [[1024, 128], [131072, 4], [512, 2], [1, 512]]))],
                         waits=[xcast[-1]] if xcast else list(gate))
        st = dma_section(p, "sync", hlt,
                         [(bass.AP(xbf.tensor, rb * 512 * 1024,
                                   [[1024, 128], [131072, 4], [512, 2], [1, 512]]),
                           bass.AP(a0scr, 0, [[16384, 128], [1024, 4], [512, 2], [1, 512]]))],
                         waits=[ld])
        xcast.append(st)
    gxt = [p.sem(pfx + "gxta"), p.sem(pfx + "gxtb")]
    gxs = [p.sem(pfx + "gxsa"), p.sem(pfx + "gxsb")]
    ii = 0
    for it in range(8):
        for tc in range(0, TF, 2048):
            w = min(2048, TF - tc)
            b = ii % 2
            stg2 = a0scr[:, 8192 + b * 2048:8192 + (b + 1) * 2048]
            tl = dma_tr_section(p, gxt[b],
                                [(stg2[:, 0:w], xbf[tc:tc + w, it * 128:(it + 1) * 128])],
                                waits=[xcast[-1], (gxs[b], gxs[b].n)], qi=0)
            dma_section(p, "scalar", gxs[b],
                        [(xT_scr[it * 128:(it + 1) * 128, tc:tc + w], stg2[:, 0:w])],
                        waits=[tl])
            ii += 1
    xtprep_done = [(gxs[0], gxs[0].n), (gxs[1], gxs[1].n)]

    NPX = [(2048, 512), (2560, 512), (3072, 512), (3584, 512), (4096, 256)]
    ev_dt = []
    prev_pan_mm = xtprep_done[0]
    prev_pan_mm2 = xtprep_done[1]
    prev_strip = xtprep_done[0]
    dt_ti = 0
    zst = []
    prev_strip_stores = None
    for ns in range(TF // 512):
        t0 = ns * 512
        xT_ok = dma_section(p, "sync", hlx,
                            [(bass.AP(a0scr, 0, [[16384, 128], [512, 8], [1, 512]]),
                              bass.AP(xT_scr.tensor, t0, [[TF, 128], [128 * TF, 8], [1, 512]]))],
                            waits=[(pe, pe.n), xtprep_done[0], xtprep_done[1]])
        pan_mm = []
        ev_xbc = []
        st = None
        for pidx, (oc0, owid) in enumerate(NPX):
            buf = pidx % 2
            wb = [pan_mm[-1]] if pidx > 0 else [prev_pan_mm, prev_pan_mm2]
            ld = dma_section(p, "sync" if buf == 0 else "scalar", glw[buf],
                             [(bass.AP(a0scr, 4096 + buf * 4096, [[16384, 128], [512, 8], [1, owid]]),
                               bass.AP(wTi.tensor, oc0, [[4864, 128], [128 * 4864, 8], [1, owid]]))],
                             waits=wb)
            for oo in range(owid // 128):
                ct = (oc0 - 2048) // 128 + oo
                bank = ps[ct % 4]
                mw = [ld, xT_ok]
                if len(pan_mm) >= 4:
                    mw.append(ev_xbc[len(pan_mm) - 4])
                else:
                    mw.append((ac, ac.n))
                    mw.append((dv, dv.n))

                def mmo(t, buf=buf, oo=oo, bank=bank):
                    last = None
                    for k in range(8):
                        last = t.matmul(bank[:, :], wpt[buf][k][:, oo * 128:(oo + 1) * 128],
                                        xTt[k], start=(k == 0), stop=(k == 7))
                    return last
                md = section(p, "tensor", pe, mmo, waits=mw)
                pan_mm.append(md)

                def ev(v, ct=ct, bank=bank):
                    return v.tensor_scalar(xbc_ot[ct], bank[:, :], bxbc[:, ct:ct + 1], None,
                                           ALU.add, ALU.bypass)
                evw = [md, evc1]
                if prev_strip_stores:
                    evw.append(prev_strip_stores)
                ev_xbc.append(section(p, "vector", dv, ev, waits=evw))
                st = dma_section(p, "gpsimd", glxb,
                                 [(xbc_raw[ct * 128:(ct + 1) * 128, t0:t0 + 512], xbc_ot[ct])],
                                 waits=[ev_xbc[-1]])

        prev_pan_mm = pan_mm[-1]

        if ns >= PRE // 512:
            for zb in range(4):
                buf = zb % 2
                ld = dma_section(p, "sync" if buf == 0 else "scalar", glw[buf],
                                 [(bass.AP(a0scr, 4096 + buf * 4096, [[16384, 128], [512, 8], [1, 512]]),
                                   bass.AP(wTi.tensor, zb * 512, [[4864, 128], [128 * 4864, 8], [1, 512]]))],
                                 waits=[(pe, pe.n)])
                for tt4 in range(4):
                    bank = ps[4 + (zb * 4 + tt4) % 2]

                    def mmz(t, buf=buf, tt4=tt4, bank=bank):
                        last = None
                        for k in range(8):
                            last = t.matmul(bank[:, :], xTt[k][:, tt4 * 128:(tt4 + 1) * 128],
                                            wpt[buf][k][:, 0:512], start=(k == 0), stop=(k == 7))
                        return last
                    mw = [ld, xT_ok, (dv, dv.n), (ac, ac.n)]
                    md = section(p, "tensor", pe, mmz, waits=mw)

                    def edz(v, bank=bank, zb=zb):
                        return v.tensor_add(evb_z[:, 0:512], bank[:, :], zrep[:, zb * 512:(zb + 1) * 512])
                    ed = section(p, "vector", dv, edz, waits=[md])

                    def slz(a, tt4=tt4, zb=zb):
                        return a.activation(y_sbt[tt4][:, zb * 512:(zb + 1) * 512],
                                            evb_z[:, 0:512], AF.Silu)
                    sd = section(p, "scalar", ac, slz, waits=[ed])
            tloc = t0 - PRE
            zst.append(dma_section(p, "sync", glz,
                                   [(bass.AP(zbuf.tensor, tloc * 2048, [[2048, 128], [262144, 4], [1, 2048]]),
                                     bass.AP(y_sb, 0, [[8192, 128], [2048, 4], [1, 2048]]))],
                                   waits=[(ac, ac.n)]))

        for tt4 in range(4):
            bank = ps[6 + tt4 % 2]

            def mmd(t, tt4=tt4, bank=bank):
                last = None
                for k in range(8):
                    last = t.matmul(bank[:, 0:512], xTt[k][:, tt4 * 128:(tt4 + 1) * 128],
                                    wdtt[k], start=(k == 0), stop=(k == 7))
                return last
            mw = [xT_ok, wdl]
            if len(ev_dt) >= 2:
                mw.append(ev_dt[-2])
            md = section(p, "tensor", pe, mmd, waits=mw)

            def evd(v, E, dt_ti=dt_ti, bank=bank):
                E(v.tensor_add(evb_z[:, 512:1024], bank[:, 0:512], drep[:, :]))
                last = v.tensor_copy(dt_tm[:, dt_ti * 32:(dt_ti + 1) * 32], evb_z[:, 512:544])
                if dt_ti * 128 >= PRE:
                    last = v.tensor_copy(dls[:, (dt_ti % 4) * 480:(dt_ti % 4 + 1) * 480],
                                         evb_z[:, 544:1024])
                return last
            ev_dt.append(section(p, "vector", dv, evd,
                                 waits=[md] + ([ev_dt[-1]] if ev_dt else [(dv, dv.n)])))
            dt_ti += 1
        if ns >= PRE // 512:
            tloc = t0 - PRE
            dma_section(p, "scalar", gldl,
                        [(bass.AP(dlbuf.tensor, tloc * 480, [[480, 128], [61440, 4], [1, 480]]),
                          bass.AP(dls, 0, [[1920, 128], [480, 4], [1, 480]]))],
                        waits=[ev_dt[-1]])
        prev_strip = (pe, pe.n)
        prev_strip_stores = (glxb, glxb.n)
    a0_done = (pe, pe.n)
    zst_final = (glz, glz.n)
    dl_final = (gldl, gldl.n)

    wol = dma_section(p, "sync", glo,
                      [(bass.AP(a0scr, 0, [[16384, 128], [1024, 16], [1, 1024]]),
                        bass.AP(wTo.tensor, 0, [[1024, 128], [128 * 1024, 16], [1, 1024]]))],
                      waits=[a0_done, (dv, dv.n), (ac, ac.n)])
    glnw = p.sem(pfx + "glnw")
    nwl = dma_section(p, "gpsimd", glnw, [(nwrep[:, :], bass.AP(bs["norm_w"].tensor, 0, [[0, 128], [1, 2048]]))],
                      waits=[a0_done, (dv, dv.n), (ac, ac.n)])

    tr_ev = []
    for ti in range(NCF):
        def trd(t, ti=ti):
            return t.transpose(pstf[0:32, 0:128], dt_tm[:, ti * 32:(ti + 1) * 32], idf[:, :])
        td = section(p, "tensor", pe, trd,
                     waits=[ev_dt[-1], (ac, ac.n), (dv, dv.n)] if ti == 0 else [tr_ev[-1]])

        def tev(v, ti=ti):
            return v.tensor_copy(dtf[:, ti * 128:(ti + 1) * 128], pstf[0:32, 0:128])
        tr_ev.append(section(p, "vector", dv, tev, waits=[td]))

    def sp(v, E):
        E(v.tensor_scalar(dtf[:, 0:TF], dtf[:, 0:TF], dtb[:, 0:1], None, ALU.add, ALU.bypass))
        E(v.tensor_scalar(spt[:, 0:TF], dtf[:, 0:TF], -1.0, None, ALU.mult, ALU.bypass))
        return v.tensor_tensor(spt[:, 0:TF], spt[:, 0:TF], dtf[:, 0:TF], ALU.max)
    spd = section(p, "vector", dv, sp, waits=[tr_ev[-1], cl])

    def spa(a, E):
        E(a.activation(spt[:, 0:TF], spt[:, 0:TF], AF.Exp, scale=-1.0))
        return a.activation(spt[:, 0:TF], spt[:, 0:TF], AF.Ln, bias=1.0)
    spa1 = section(p, "scalar", ac, spa, waits=[spd])

    def sp2(v, E):
        E(v.tensor_scalar_max(dtf[:, 0:TF], dtf[:, 0:TF], 0.0))
        E(v.tensor_tensor(dtf[:, 0:TF], dtf[:, 0:TF], spt[:, 0:TF], ALU.add))
        E(v.tensor_scalar(gpl[:, 0:TF], dtf[:, 0:TF], negA[:, 0:1], None, ALU.mult, ALU.bypass))
        return v.memset(zerop[:, 0:TF], 0.0)
    sp2d = section(p, "vector", dv, sp2, waits=[spa1, eA2])

    def scan(v):
        return v.tensor_tensor_scan(gcf[:, 0:TF], gpl[:, 0:TF], zerop[:, 0:TF], 0.0, ALU.add, ALU.add)
    scand = section(p, "vector", dv, scan, waits=[sp2d])

    g1view = bass.AP(plL, 127, [[4096, 32], [128, NCF], [0, 128]])
    g0view = bass.AP(plL, 127, [[4096, 32], [128, NCF - 1], [0, 128]])

    def uden(v, E):
        E(v.tensor_copy(uf[:, 0:128], gcf[:, 0:128]))
        E(v.tensor_sub(bass.AP(plL, 32 * 4096 + 128, [[4096, 32], [128, NCF - 1], [1, 128]]),
                       bass.AP(plL, 128, [[4096, 32], [128, NCF - 1], [1, 128]]), g0view, ALU.subtract)
          if False else
          v.tensor_tensor(bass.AP(plL, 32 * 4096 + 128, [[4096, 32], [128, NCF - 1], [1, 128]]),
                          bass.AP(plL, 128, [[4096, 32], [128, NCF - 1], [1, 128]]), g0view, ALU.subtract))
        E(v.tensor_tensor(spt[:, 0:TF], gcf[:, 0:TF], g1view, ALU.subtract))
        E(v.tensor_copy(g1all[:, 0:NCF], bass.AP(plL, 127, [[4096, 32], [128, NCF]])))
        return v.tensor_scalar(g1all[:, 32:32 + NCF], g1all[:, 0:NCF], -1.0, None, ALU.mult, ALU.bypass)
    ud = section(p, "vector", dv, uden, waits=[scand])

    def dbea(a):
        return a.activation(spt[:, 0:TF], spt[:, 0:TF], AF.Exp, scale=-1.0)
    dbe1 = section(p, "scalar", ac, dbea, waits=[ud])

    def dbev(v):
        return v.tensor_tensor(dbef[:, 0:TF], dtf[:, 0:TF], spt[:, 0:TF], ALU.mult)
    dbe2 = section(p, "vector", dv, dbev, waits=[dbe1])

    trq = [dbe2]
    for (plane, base, dst) in [(plL, 32, u_tm), (plB, 64, dbe_tm), (plA, 0, dt_tm)]:
        for ti in range(NCF):
            def trp(t, plane=plane, base=base, ti=ti):
                src = bass.AP(plane, base * 4096 + ti * 128, [[4096, 32], [1, 128]])
                return t.transpose(pstf[:, 0:32], src, idf[base:base + 32, base:base + 32])
            td = section(p, "tensor", pe, trp, waits=[trq[-1]])

            def tev(v, dst=dst, ti=ti):
                return v.tensor_copy(dst[:, ti * 32:(ti + 1) * 32], pstf[:, 0:32])
            trq.append(section(p, "vector", dv, tev, waits=[td]))
    a2_tm = trq[-1]

    events = []
    eid = {}
    for i in range(NCF):
        for (k, le, re) in sched[i]["merges"]:
            eid[(i, k)] = len(events)
            events.append((i, k, le, re))
    NEV = len(events)
    if NEV:
        def emtf(v):
            last = None
            for e, (i, k, le, re) in enumerate(events):
                last = v.tensor_tensor(emt[:, e:e + 1], g1all[:, i:i + 1],
                                       g1all[:, 32 + le:33 + le], ALU.add)
            return last
        em1 = section(p, "vector", dv, emtf, waits=[ud])

        def emta(a):
            return a.activation(emt[:, 0:NEV], emt[:, 0:NEV], AF.Exp)
        em2 = section(p, "scalar", ac, emta, waits=[em1])
        efl = dma_section(p, "gpsimd", glu,
                          [(uflat[0:1, 0:32 * NEV], bass.AP(emt, 0, [[32, 32], [1, NEV]]))],
                          waits=[em2, a2_tm])
        emr = []
        for j in range((32 * NEV + 511) // 512):
            w = min(512, 32 * NEV - j * 512)

            def emmj(t, j=j, w=w):
                return t.matmul(ps[5][:, 0:w], onesf[64:65, :], uflat[0:1, j * 512:j * 512 + w],
                                start=True, stop=True)
            mj = section(p, "tensor", pe, emmj,
                         waits=[efl, odone, a2_tm] if j == 0 else [emr[-1]])

            def evj(v, j=j, w=w):
                return v.tensor_copy(emrep[:, j * 512:j * 512 + w], ps[5][:, 0:w])
            emr.append(section(p, "vector", dv, evj, waits=[mj]))
        emdone = emr[-1]
    else:
        emdone = ud

    xbc_final = (glxb, glxb.n)
    last_store = (hl, 0)
    clamp_of = {}
    expd_of = {}
    ltf_of = {}
    attf_of = {}
    attT_of = {}
    attTe_of = {}
    vpA_of = {}
    vpB_of = {}
    ymm_of = {}
    ydf_of = {}
    ppm_of = {}
    ppl_of = {}
    ppe_last_of = {}
    gate_of = {}
    lfm_of = {}
    lme_of = {}
    decm_of = {}
    lme_last = {}
    gctr = [0]
    bmtp = ps[4][:, 448:512].bitcast(BF16)
    rhoP = [rho, xbc_c.bitcast(F32)[:, 1024:1216]]
    for sb in range(NSB):
        t0 = sb * 512
        own = sb >= SB0
        nct = 18 if own else 17
        conv_last = []
        for cg in range(3):
            c_lo = cg * 6
            c_hi = min(nct, c_lo + 6)
            if c_hi <= c_lo:
                continue
            halo = 3 if sb > 0 else 0
            nt_ = c_hi - c_lo
            lds = [dma_section(p, "sync", glxc,
                               [(bass.AP(xbc_c, 3 - halo, [[516 * 6, 128], [516, nt_], [1, 512 + halo]]),
                                 bass.AP(xbc_raw.tensor, c_lo * 128 * TF + t0 - halo,
                                         [[TF, 128], [128 * TF, nt_], [1, 512 + halo]]))],
                               waits=[xbc_final, (pe, pe.n), (ac, ac.n), (dv, dv.n), (po, po.n)])]
            if sb == 0 and cg == 0:
                mspad = section(p, "gpsimd", gm, lambda g: g.memset(
                    bass.AP(xbc_c, 0, [[516 * 6, 128], [516, 6], [1, 3]]), 0.0), waits=[])
            hmd = None
            if sb == SB0:
                def hmask(v, c_lo=c_lo, c_hi=c_hi):
                    last = None
                    for c in range(c_lo, c_hi):
                        last = v.tensor_scalar(xbc_ct[c - c_lo][:, 0:3], xbc_ct[c - c_lo][:, 0:3],
                                               hmc[:, 0:1], None, ALU.mult, ALU.bypass)
                    return last
                hmd = section(p, "vector", dv, hmask, waits=[lds[-1], cl])
            for c in range(c_lo, c_hi):
                ci = c - c_lo
                sl = [xbc_ct[ci][:, jj:jj + 512] for jj in range(4)]

                def cv1(v, c=c, sl=sl):
                    v.tensor_scalar(cta[:, :], sl[0], convw[:, 4 * c:4 * c + 1], None, ALU.mult, ALU.bypass)
                    v.tensor_scalar(ctb[:, :], sl[1], convw[:, 4 * c + 1:4 * c + 2], None,
                                    ALU.mult, ALU.bypass)
                    return v.tensor_scalar(ctc[:, :], sl[2], convw[:, 4 * c + 2:4 * c + 3], None,
                                           ALU.mult, ALU.bypass)
                w1_ = [lds[-1], cl]
                if conv_last:
                    w1_.append(conv_last[-1][1])
                if sb == 0 and cg == 0:
                    w1_.append(mspad)
                if hmd is not None:
                    w1_.append(hmd)
                c1 = section(p, "vector", dv, cv1, waits=w1_)

                def cv2(g, E, c=c):
                    E(g.tensor_tensor(cta[:, :], cta[:, :], ctb[:, :], ALU.add))
                    return g.tensor_tensor(cta[:, :], cta[:, :], ctc[:, :], ALU.add)
                c2_ = section(p, "gpsimd", po, cv2, waits=[c1])

                def cv1b(v, c=c, sl=sl):
                    return v.tensor_scalar(ctb[:, :], sl[3], convw[:, 4 * c + 3:4 * c + 4], None,
                                           ALU.mult, ALU.bypass)
                c2b = section(p, "vector", dv, cv1b, waits=[c2_])

                def cv2b(g, c=c):
                    return g.tensor_tensor(cta[:, :], cta[:, :], ctb[:, :], ALU.add)
                c3 = section(p, "gpsimd", po, cv2b, waits=[c2b])

                def cv4(a, c=c, sb=sb):
                    return a.activation(xbc_ot[c], cta[:, :], AF.Silu, scale=smcol[:, sb:sb + 1])
                c4 = section(p, "scalar", ac, cv4, waits=[c3])
                conv_last.append((c3, c4))
        conv_done = conv_last[-1][1]

        xtr = []
        for q in range(4):
            for c in range(16):
                def trx(t, q=q, c=c):
                    return t.transpose(pstb[:, 0:128], xbc_ot[c][:, q * 128:(q + 1) * 128], idn[:, :])
                w = [conv_last[c][1]] if not xtr else [xtr[-1]]
                if not xtr:
                    w.append((dv, dv.n))
                    w.append(conv_done)
                td = section(p, "tensor", pe, trx, waits=w)

                def tev(v, q=q, c=c):
                    return v.tensor_copy(x_tmt[q][:, c * 128:(c + 1) * 128], pstb[:, 0:128])
                xtr.append(section(p, "vector", dv, tev, waits=[td]))
        xtm_done = xtr[-1]

        if own:
            tloc = t0 - PRE
            zl = dma_section(p, "sync", glz,
                             [(bass.AP(z_sb, 0, [[8192, 128], [2048, 4], [1, 2048]]),
                               bass.AP(zbuf.tensor, tloc * 2048, [[2048, 128], [262144, 4], [1, 2048]]))],
                             waits=[zst_final, (dv, dv.n), (pe, pe.n)])
            dll = dma_section(p, "scalar", gldl,
                              [(bass.AP(dls, 0, [[1920, 128], [480, 4], [1, 480]]),
                                bass.AP(dlbuf.tensor, tloc * 480, [[480, 128], [61440, 4], [1, 480]]))],
                              waits=[dl_final, (dv, dv.n)])
            lam_ev = []
            for q in range(4):
                def lm(v, E, q=q):
                    lv = lam_tm[:, q * 480:(q + 1) * 480]
                    E(v.tensor_tensor(lv, dls[:, q * 480:(q + 1) * 480], lrep[:, :], ALU.mult))
                    E(v.tensor_scalar_mul(evb_z[:, 0:480], lv, -1.0))
                    return v.tensor_tensor(evb_z[:, 0:480], evb_z[:, 0:480], lv, ALU.max)
                lw0 = [lam_ev[-1]] if lam_ev else [dll, a2_tm, (po, po.n)]
                l1 = section(p, "vector", dv, lm, waits=lw0)

                def lma(a, E, q=q):
                    E(a.activation(evb_z[:, 512:992], evb_z[:, 0:480], AF.Exp, scale=-1.0))
                    return a.activation(evb_z[:, 512:992], evb_z[:, 512:992], AF.Ln, bias=1.0)
                l2 = section(p, "scalar", ac, lma, waits=[l1])

                def lm2(v, E, q=q):
                    lv = lam_tm[:, q * 480:(q + 1) * 480]
                    E(v.tensor_scalar_max(lv, lv, 0.0))
                    return v.tensor_tensor(lv, lv, evb_z[:, 512:992], ALU.add)
                lam_ev.append(section(p, "vector", dv, lm2, waits=[l2]))
            lam_done = lam_ev[-1]

        base = sb * 128
        uinfo = []
        for q in range(4):
            i = sb * 4 + q
            uinfo.append((q, i, sched[i]["segs"], sched[i]["merges"], sched[i]["slot"]))
        qk_ev_q = {}
        bm_ev_q = {}
        ufl_q = {}
        rho_done_q = {}
        first_decm = [True]

        def emit_decm(n):
            qq = (n - base) // 32
            ii = sb * 4 + qq
            hh = (n - base) % 32
            pg = gctr[0] % 2
            gctr[0] += 1

            def decm(t, hh=hh, pg=pg):
                return t.matmul(ps[pg][:, 0:512], onesf[64:65, 0:128],
                                uflat[0:1, hh * 128:(hh + 4) * 128], start=True, stop=True)
            dw = [ufl_q[qq], odone]
            if n - 5 in clamp_of:
                dw.append(clamp_of[n - 5])
            if first_decm[0]:
                dw.append((dv, dv.n))
                first_decm[0] = False
            d = section(p, "tensor", pe, decm, waits=dw)
            for mm in range(4):
                decm_of[n + mm] = (d, pg)

        for k in range(128 + 8):
            n = base + k
            ku = k < 128
            if ku:
                q, i, segs, merges, fslot = uinfo[k // 32]
                h = k % 32
            if ku and h == 0 and own:
                ufl_q[q] = dma_section(p, "gpsimd", glu,
                                       [(uflat[0:1, 0:4096],
                                         bass.AP(plL, 32 * 4096 + i * 128, [[4096, 32], [1, 128]]))],
                                       waits=[decm_of[n - 4][0]] if (n - 4) in decm_of else [(pe, pe.n)])
                rhob = rhoP[q % 2]
                rho_ev = []
                for bidx, (k_, qend) in enumerate(segs):
                    def scs(a, qend=qend, i=i):
                        return a.activation(scsc[:, :], gcf[:, i * 128:(i + 1) * 128], AF.Exp,
                                            bias=g1all[:, 32 + qend:33 + qend])
                    if bidx == 0:
                        sw = [a2_tm, lam_done]
                        if (q - 1) in rho_done_q:
                            sw.append(rho_done_q[q - 1])
                        pv = base + (q - 2) * 32 + 31
                        if pv in gate_of:
                            sw.append(gate_of[pv])
                    else:
                        sw = [rho_ev[-1]]
                    sd = section(p, "scalar", ac, scs, waits=sw)

                    def sct(v, E, bidx=bidx, k_=k_, q=q, rhob=rhob):
                        ii_ = None
                        for jb in range(4):
                            ii_ = v.transpose(scT[32 * jb:32 * jb + 32, bidx * 32:(bidx + 1) * 32],
                                              scsc[0:32, 32 * jb:32 * jb + 32])
                        E(ii_)
                        return v.tensor_tensor(rhob[:, bidx * 32:(bidx + 1) * 32],
                                               scT[:, bidx * 32:(bidx + 1) * 32],
                                               bass.AP(lam_tm, q * 480 + 8 + k_, [[1920, 128], [15, 32]]),
                                               ALU.mult)
                    rho_ev.append(section(p, "vector", dv, sct, waits=[sd]))
                rho_done_q[q] = rho_ev[-1] if segs else (dv, 0)
                emit_decm(n)
            elif ku and own and h % 4 == 0 and h > 0:
                emit_decm(n)

            if 3 <= k <= 130:
                m = n - 3
                km = k - 3
                mq, mh, mpar = km // 32, km % 32, km % 4
                mi = sb * 4 + mq
                if own:
                    def ymm(t, mpar=mpar, mb=2 + (km % 2)):
                        return t.matmul(ps[mb][:, 0:64],
                                        atT[:, mpar * 128:(mpar + 1) * 128],
                                        vp2[:, mpar * 128:mpar * 128 + 64], start=True, stop=True)
                    yw = [attTe_of[m], vpA_of[m]]
                    if m - 2 in ydf_of:
                        yw.append(ydf_of[m - 2])
                    ymm_of[m] = section(p, "tensor", pe, ymm, waits=yw)

            if own and 4 <= k <= 131:
                m = n - 4
                km = k - 4
                mq, mh, mpar = km // 32, km % 32, km % 4
                msegs = uinfo[mq][2]
                mb = 2 + (km % 2)
                ppl = []
                for s_, (k_, qend) in enumerate(msegs):
                    pw = [rho_done_q[mq], conv_done]
                    if mh in lme_last:
                        pw.append(lme_last[mh])

                    def ppm(t, mq=mq, mh=mh, k_=k_, mb=mb, col=64 + 64 * s_):
                        return t.matmul(ps[mb][:, col:col + 64],
                                        xbc_ot[17][:, mq * 128:(mq + 1) * 128],
                                        slot_ap(mh, k_), start=True, stop=True)
                    pmd = section(p, "tensor", pe, ppm, waits=pw)
                    ppl.append((pmd, s_))
                ppm_of[m] = ppl[-1][0] if ppl else None
                ppl_of[m] = ppl

            if 5 <= k <= 132:
                m = n - 5
                km = k - 5
                mq, mh, mpar = km // 32, km % 32, km % 4
                mi = sb * 4 + mq
                if mi != NCF - 1:
                    mmerges, mfslot = uinfo[mq][3], uinfo[mq][4]

                    def lfm(t, mpar=mpar, lb=5 + 2 * (km % 2)):
                        return t.matmul(ps[lb][:, 0:64], bmt_sb[:, :],
                                        vp2[:, mpar * 128 + 64:mpar * 128 + 128], start=True, stop=True)
                    lw = [bm_ev_q[mq], vpB_of[m]]
                    if m - 2 in lme_of:
                        lw.append(lme_of[m - 2])
                    lfm_of[m] = section(p, "tensor", pe, lfm, waits=lw)

            if 2 <= k <= 131:
                wB = []
                eT = k <= 129 and own
                eY = own and 4 <= k
                eV = 3 <= k <= 130
                if eT:
                    aT_m = n - 2
                    aT_par = (k - 2) % 4
                    wB.append(attT_of[aT_m])
                    if aT_m - 4 in ymm_of:
                        wB.append(ymm_of[aT_m - 4])
                if eY:
                    ym = n - 4
                    ykm = k - 4
                    ymq, ymh, ympar = ykm // 32, ykm % 32, ykm % 4
                    ymb = 2 + (ykm % 2)
                    yppl = ppl_of.pop(ym)
                    yrhob = rhoP[ymq % 2]
                    wB.append(ymm_of[ym])
                    if yppl:
                        wB.append(yppl[-1][0])
                    if ym - 4 in gate_of:
                        wB.append(gate_of[ym - 4])
                vm = n - 3
                vkm = k - 3
                vmq, vmh, vmpar = vkm // 32, vkm % 32, vkm % 4
                vmi = sb * 4 + vmq
                eV = eV and vmi != NCF - 1
                if eV:
                    wB.append(xtm_done)
                    wB.append(a2_tm)
                    if vm - 4 in lfm_of:
                        wB.append(lfm_of[vm - 4])

                def laneB(v, E):
                    last = None
                    if eT:
                        last = v.tensor_copy(atT[:, aT_par * 128:(aT_par + 1) * 128], pstb[:, 0:128])
                    if eY:
                        if last is not None:
                            E(last)
                        last = v.scalar_tensor_tensor(yh[:, ympar * 64:(ympar + 1) * 64],
                                                      x_tmt[ymq][:, ymh * 64:(ymh + 1) * 64],
                                                      dcols[:, ymh:ymh + 1],
                                                      ps[ymb][:, 0:64],
                                                      ALU.mult, ALU.add)
                        for (pmd_, s_) in yppl:
                            E(last)
                            last = v.scalar_tensor_tensor(yh[:, ympar * 64:(ympar + 1) * 64],
                                                          ps[ymb][:, 64 + 64 * s_:128 + 64 * s_],
                                                          yrhob[:, s_ * 32 + ymh:s_ * 32 + ymh + 1],
                                                          yh[:, ympar * 64:(ympar + 1) * 64],
                                                          ALU.mult, ALU.add)
                    if eV:
                        if last is not None:
                            E(last)
                        last = v.tensor_scalar(vp2[:, vmpar * 128 + 64:vmpar * 128 + 128],
                                               x_tmt[vmq][:, vmh * 64:(vmh + 1) * 64],
                                               dbe_tm[:, vmi * 32 + vmh:vmi * 32 + vmh + 1], None,
                                               ALU.mult, ALU.bypass)
                    return last
                if eT or eY or eV:
                    hB = section(p, "vector", dv, laneB, waits=wB)
                    if eT:
                        attTe_of[aT_m] = hB
                    if eY:
                        ydf_of[ym] = hB
                        ppe_last_of[ym] = hB

                        def gatef(v, ymh=ymh, ympar=ympar, ymq=ymq):
                            return v.tensor_tensor(y_sbt[ymq][:, ymh * 64:(ymh + 1) * 64],
                                                   yh[:, ympar * 64:(ympar + 1) * 64],
                                                   z_sbt[ymq][:, ymh * 64:(ymh + 1) * 64], ALU.mult)
                        gate_of[ym] = section(p, "gpsimd", po, gatef, waits=[hB, zl])
                    if eV:
                        vpB_of[vm] = hB
            if ku and own:
                par = k % 4
                p2 = k % 2
                pgn = decm_of[n][1]
                wA = [decm_of[n][0], a2_tm, lam_done, cl, xtm_done]
                if n - 2 in expd_of:
                    wA.append(expd_of[n - 2])
                if n - 2 in attf_of:
                    wA.append(attf_of[n - 2])
                if n - 4 in ymm_of:
                    wA.append(ymm_of[n - 4])

                def laneA(v, E, h=h, p2=p2, par=par, i=i, q=q, pg=pgn):
                    c_ = v.tensor_scalar(dti[:, p2 * 128:(p2 + 1) * 128],
                                         ps[pg][:, (h % 4) * 128:(h % 4 + 1) * 128],
                                         u_tm[:, i * 32 + h:i * 32 + h + 1], 0.0,
                                         ALU.subtract, ALU.max)
                    E(c_)
                    lv = ltl[:, p2 * 128:(p2 + 1) * 128]
                    last = v.tensor_scalar(lv, maskt[0], lam_tm[:, q * 480 + 15 * h:q * 480 + 15 * h + 1],
                                           None, ALU.mult, ALU.bypass)
                    for l in range(1, 8):
                        E(last)
                        last = v.scalar_tensor_tensor(lv, maskt[l],
                                                      lam_tm[:, q * 480 + 15 * h + l:q * 480 + 15 * h + l + 1],
                                                      lv, ALU.mult, ALU.add)
                    E(last)
                    return v.tensor_scalar(vp2[:, par * 128:par * 128 + 64],
                                           x_tmt[q][:, h * 64:(h + 1) * 64],
                                           dt_tm[:, i * 32 + h:i * 32 + h + 1], None,
                                           ALU.mult, ALU.bypass)
                hA = section(p, "vector", dv, laneA, waits=wA)
                clamp_of[n] = hA
                ltf_of[n] = hA
                vpA_of[n] = hA

                def expd(a, p2=p2):
                    return a.activation(dec[:, p2 * 128:(p2 + 1) * 128],
                                        dti[:, p2 * 128:(p2 + 1) * 128], AF.Exp, scale=-1.0)
                ew = [hA]
                if n - 2 in attf_of:
                    ew.append(attf_of[n - 2])
                expd_of[n] = section(p, "scalar", ac, expd, waits=ew)
            if own and 1 <= k <= 128:
                m = n - 1
                mp2 = (k - 1) % 2
                mq = (k - 1) // 32

                def attf(v, E, mp2=mp2):
                    E(v.tensor_tensor(attb[:, mp2 * 128:(mp2 + 1) * 128], qk_sb[:, :],
                                      dec[:, mp2 * 128:(mp2 + 1) * 128], ALU.mult))
                    return v.tensor_tensor(attb[:, mp2 * 128:(mp2 + 1) * 128],
                                           attb[:, mp2 * 128:(mp2 + 1) * 128],
                                           ltl[:, mp2 * 128:(mp2 + 1) * 128], ALU.mult)
                aw = [expd_of[m], ltf_of[m], qk_ev_q[mq]]
                if m - 2 in attT_of:
                    aw.append(attT_of[m - 2])
                attf_of[m] = section(p, "gpsimd", po, attf, waits=aw)

            if ku and h == 0 and own:
                def qkm(t, q=q):
                    return t.matmul(ps[4][:, 0:128], xbc_ot[17][:, q * 128:(q + 1) * 128],
                                    xbc_ot[16][:, q * 128:(q + 1) * 128], start=True, stop=True)
                qkw = [conv_done]
                if (q - 1) in qk_ev_q:
                    qkw.append(qk_ev_q[q - 1])
                if (q - 1) in bm_ev_q:
                    qkw.append(bm_ev_q[q - 1])
                qkd = section(p, "tensor", pe, qkm, waits=qkw)

                def qke(v):
                    return v.tensor_copy(qk_sb[:, :], ps[4][:, 0:128])
                kw = [qkd]
                if n - 1 in attf_of:
                    kw.append(attf_of[n - 1])
                qk_ev_q[q] = section(p, "vector", dv, qke, waits=kw)

            if own and 1 <= k <= 128:
                m = n - 1

                def attT(t, mp2=(k - 1) % 2):
                    return t.transpose(pstb[:, 0:128], attb[:, mp2 * 128:(mp2 + 1) * 128], idn[:, :])
                tw = [attf_of[m]]
                if m - 1 in attTe_of:
                    tw.append(attTe_of[m - 1])
                if m == base:
                    tw.append(xtm_done)
                attT_of[m] = section(p, "tensor", pe, attT, waits=tw)

            if 5 <= k <= 132:
                m = n - 5
                km = k - 5
                mq, mh, mpar = km // 32, km % 32, km % 4
                mi = sb * 4 + mq
                if mi != NCF - 1:
                    mmerges, mfslot = uinfo[mq][3], uinfo[mq][4]
                    def lme(v, E, mh=mh, mi=mi, mmerges=mmerges, mfslot=mfslot,
                            lb=5 + 2 * (km % 2)):
                        lp = ps[lb][:, 0:64]
                        if not mmerges:
                            return v.tensor_copy(slot_ap(mh, 0), lp)
                        last = None
                        for zi, (k_, le, re) in enumerate(mmerges):
                            e = eid[(mi, k_)]
                            acc = lp if zi == 0 else slot_ap(mh, mfslot)
                            if last is not None:
                                E(last)
                            last = v.scalar_tensor_tensor(slot_ap(mh, mfslot), slot_ap(mh, k_),
                                                          emrep[:, mh * NEV + e:mh * NEV + e + 1],
                                                          acc, ALU.mult, ALU.add)
                        return last
                    lwm = [lfm_of[m], emdone]
                    if own and ppm_of.get(m) is not None:
                        lwm.append(ppm_of[m])
                    if mh in lme_last:
                        lwm.append(lme_last[mh])
                    ld_ = section(p, "vector", dv, lme, waits=lwm)
                    lme_of[m] = ld_
                    lme_last[mh] = ld_

            if k >= 4 and (k - 4) % 32 == 0 and k - 4 < 128:
                q2 = (k - 4) // 32

                def bmm(t, q2=q2):
                    return t.transpose(bmtp[:, 0:128], xbc_ot[16][:, q2 * 128:(q2 + 1) * 128], idn[:, :])
                bmw = [conv_done]
                if (q2 - 1) in bm_ev_q:
                    bmw.append(bm_ev_q[q2 - 1])
                else:
                    bmw.append(xtm_done)
                if q2 in qk_ev_q:
                    bmw.append(qk_ev_q[q2])
                bmd = section(p, "tensor", pe, bmm, waits=bmw)

                def bme(v):
                    return v.tensor_copy(bmt_sb[:, :], bmtp[:, 0:128])
                kw2 = [bmd]
                lastq = base + q2 * 32 - 1
                if lastq in lfm_of:
                    kw2.append(lfm_of[lastq])
                bm_ev_q[q2] = section(p, "vector", dv, bme, waits=kw2)

        if own:
            rms_ev = []
            for q in range(4):
                def sqf(a, q=q):
                    return a.activation(sqs_scr[:, 0:2048], y_sbt[q], AF.Square,
                                        accum_out=rmsc[:, q:q + 1])
                sq1 = section(p, "scalar", ac, sqf,
                              waits=[gate_of[sb * 128 + 127], gate_of[sb * 128 + 126]] if q == 0 else [rms_ev[-1]])

                def rst(v, q=q):
                    return v.tensor_scalar(rmsc[:, q:q + 1], rmsc[:, q:q + 1], 1.0 / 2048.0,
                                           RMS_EPS, ALU.mult, ALU.add)
                r1 = section(p, "vector", dv, rst, waits=[sq1])

                def ras(a, E, q=q):
                    E(a.activation(rmsc[:, q:q + 1], rmsc[:, q:q + 1], AF.Ln))
                    return a.activation(rmsc[:, q:q + 1], rmsc[:, q:q + 1], AF.Exp, scale=-0.5)
                r2 = section(p, "scalar", ac, ras, waits=[r1])

                def yzn(v, E, q=q):
                    E(v.tensor_scalar(y_sbt[q], y_sbt[q], rmsc[:, q:q + 1], None, ALU.mult, ALU.bypass))
                    return v.tensor_tensor(y_sbt[q], y_sbt[q], nwrep[:, :], ALU.mult)
                rms_ev.append(section(p, "vector", dv, yzn, waits=[r2, nwl]))
            ytr = []
            for q in range(4):
                for c in range(16):
                    def trn(t, q=q, c=c):
                        return t.transpose(pstb[:, 0:128], y_sbt[q][:, c * 128:(c + 1) * 128], idn[:, :])
                    w = [rms_ev[q]] + ([ytr[-1]] if ytr else [])
                    td = section(p, "tensor", pe, trn, waits=w)

                    def tev(v, q=q, c=c):
                        return v.tensor_copy(ypTt[c][:, q * 128:(q + 1) * 128], pstb[:, 0:128])
                    ytr.append(section(p, "vector", dv, tev, waits=[td]))
            ytr_done = ytr[-1]

            xtr2 = []
            for j in range(4):
                xrl = dma_section(p, "sync", hlx,
                                  [(evb_z[:, 0:1024], xin_d[t0 + j * 128:t0 + (j + 1) * 128, :])],
                                  waits=[last_store, (pe, pe.n), (dv, dv.n)] if j == 0 else [xtr2[-1]])
                for c in range(8):
                    def trr(t, c=c):
                        return t.transpose(pstf[:, 0:128], evb_z[:, c * 128:(c + 1) * 128], idf[:, :])
                    w = [xrl] if c == 0 else [xtr2[-1]]
                    if j == 0 and c == 0:
                        w.append(ytr_done)
                    td = section(p, "tensor", pe, trr, waits=w)

                    def tev(v, j=j, c=c):
                        return v.tensor_copy(xftB[c][:, j * 128:(j + 1) * 128], pstf[:, 0:128])
                    xtr2.append(section(p, "vector", dv, tev, waits=[td]))
            xres_done = xtr2[-1]

            op_ev = []
            for o in range(8):
                def mop(t, o=o):
                    last = None
                    for k in range(16):
                        last = t.matmul(ps[o][:, :], wopt[k][:, o * 128:(o + 1) * 128], ypTt[k],
                                        start=(k == 0), stop=(k == 15))
                    return last
                w = [ytr_done, wol, xres_done]
                if o == 0:
                    w.append((dv, dv.n))
                    w.append((ac, ac.n))
                md = section(p, "tensor", pe, mop, waits=w)

                def evo(v, o=o):
                    return v.scalar_tensor_tensor(xftB[o], ps[o][:, :], bop[:, o:o + 1], xftB[o],
                                                  ALU.add, ALU.add)
                op_ev.append(section(p, "vector", dv, evo, waits=[md, evc1]))
            last_store = emit_ln_fm(p, pfx, xftB, xout_d, t0 - PRE, 512, ones, op_ev[-1], [],
                                    lnb, statB, ps, pe, dv, ac, hl)
    return last_store




def make_b2_program(town=TOWN, debug=False):
    nc = bass.Bass("TRN2", target_bir_lowering=False)
    xin = nc.dram_tensor("xin", [DM, town], F32, kind="ExternalInput").ap()
    k0 = nc.dram_tensor("k0", [2048, 4096], U8, kind="ExternalInput").ap()
    k1 = nc.dram_tensor("k1", [4096, 4096], U8, kind="ExternalInput").ap()
    k2 = nc.dram_tensor("k2", [4096, 1024], U8, kind="ExternalInput").ap()
    s0 = nc.dram_tensor("s0", [1], F32, kind="ExternalInput").ap()
    s1 = nc.dram_tensor("s1", [1], F32, kind="ExternalInput").ap()
    s2 = nc.dram_tensor("s2", [1], F32, kind="ExternalInput").ap()
    xout = nc.dram_tensor("xout", [DM, town], F32, kind="ExternalOutput").ap()
    p = Prog(nc)
    p.open_block()
    if debug:
        p.dbg_y = nc.dram_tensor("d_y", [128, 512], F32, kind="ExternalOutput").ap()
        p.dbg_mm = nc.dram_tensor("d_mm", [128, 512], F32, kind="ExternalOutput").ap()
        p.dbg_mm_sb = p.sb("dbgmmsb", [128, 512], F32)
        p.dbg_xf = nc.dram_tensor("d_xf", [128, 512], F32, kind="ExternalOutput").ap()
        p.dbg_xf2 = nc.dram_tensor("d_xf2", [128, 512], F32, kind="ExternalOutput").ap()
        p.dbg_rs2 = nc.dram_tensor("d_rs2", [128, 512], F32, kind="ExternalOutput").ap()
    build_b2(p, xin, k0, k1, k2, s0, s1, s2, xout, town=town)
    if debug:
        dbg = {"d_h": nc.dram_tensor("d_h", [128, 512], F32, kind="ExternalOutput").ap(),
               "d_rs1": nc.dram_tensor("d_rs1", [128, 512], F32, kind="ExternalOutput").ap(),
               "d_h2": nc.dram_tensor("d_h2", [128, 512], F32, kind="ExternalOutput").ap(),
               "d_c": nc.dram_tensor("d_c", [128, 5], F32, kind="ExternalOutput").ap()}
        p2 = None
        hb = p._sb["b2h"]
        h2b = p._sb["b2h2"]
        cs = p._sb["b2c"]
        rs1 = p._sb["b2rs1"]
        dsem = p.sem("dbg")
        dma_section(p, "gpsimd", dsem, [(dbg["d_h"], hb[:, 0:512]),
                                        (dbg["d_rs1"], rs1[:, 0:512]),
                                        (dbg["d_h2"], h2b[:, 0:512]),
                                        (dbg["d_c"], cs[:, 0:5])])
    finish(p)
    p.close()
    return nc



_PROGS = {}


def _get_prog(name, builder):
    if name not in _PROGS:
        _PROGS[name] = builder()
    return _PROGS[name]


def kernel_unfused(**inp):
    x = np.ascontiguousarray(inp["L0_attn_hidden_states"], dtype=np.float32)
    zeros = np.zeros((TOWN, DM), np.float32)
    sb1 = np.ones(16, np.float32)
    sb0 = np.ones(16, np.float32)
    sb0[0:T // 2 // 512] = 0.0
    one = np.ones(1, np.float32)
    zero = np.zeros(1, np.float32)

    nc1 = _get_prog("b1", lambda: make_b1_program(tseq=T))
    maps = []
    for c in range(NCORES):
        b, half = c // 2, c % 2
        xin = x[b] if half == 1 else np.concatenate([zeros, x[b, :TOWN]], 0)
        maps.append({"xin": np.ascontiguousarray(xin),
                     "w_in": inp["L0_attn_in_proj_w"], "w_out": inp["L0_attn_out_proj_w"],
                     "in_proj_b": inp["L0_attn_in_proj_b"], "conv_w": inp["L0_attn_conv_w"],
                     "dt_bias": inp["L0_attn_dt_bias"], "A_log": inp["L0_attn_A_log"],
                     "L": inp["L0_attn_L"], "D": inp["L0_attn_D"],
                     "norm_w": inp["L0_attn_norm_w"], "out_proj_b": inp["L0_attn_out_proj_b"],
                     "sbmask": sb1 if half == 1 else sb0,
                     "hmask": one if half == 1 else zero})
    r1 = run_bass_kernel_spmd(nc1, maps, core_ids=list(range(NCORES))).results
    x1 = [r["xout"] for r in r1]

    nc2 = _get_prog("b2", make_b2_program)
    k0 = np.ascontiguousarray(inp["L0_ffn_k0"]).view(np.uint8)
    k1 = np.ascontiguousarray(inp["L0_ffn_k1"]).view(np.uint8)
    k2 = np.ascontiguousarray(inp["L0_ffn_k2"]).view(np.uint8)
    maps = [{"xin": x1[c], "k0": k0, "k1": k1, "k2": k2,
             "s0": np.asarray(inp["L0_ffn_s0"], np.float32).reshape(1),
             "s1": np.asarray(inp["L0_ffn_s1"], np.float32).reshape(1),
             "s2": np.asarray(inp["L0_ffn_s2"], np.float32).reshape(1)}
            for c in range(NCORES)]
    r2 = run_bass_kernel_spmd(nc2, maps, core_ids=list(range(NCORES))).results
    x2 = [r["xout"] for r in r2]

    nc3 = _get_prog("b3", make_b3_program)
    maps = [{"xin": x2[c], "wqkv": inp["L1_attn_W_qkv"], "wproj": inp["L1_attn_W_proj"],
             "bproj": inp["L1_attn_b_proj"]} for c in range(NCORES)]
    r3 = run_bass_kernel_spmd(nc3, maps, core_ids=list(range(NCORES))).results
    x3 = [r["xout"] for r in r3]

    nc4 = _get_prog("b4", make_b4_program)
    maps = [{"xin": x3[c], "w1": inp["L1_ffn_w1_w"], "w2": inp["L1_ffn_w2_w"],
             "h1": inp["L1_ffn_h1_w"], "h2": inp["L1_ffn_h2_w"], "w3": inp["L1_ffn_w3_w"],
             "w1s": inp["L1_ffn_w1_s"], "w2s": inp["L1_ffn_w2_s"], "h1s": inp["L1_ffn_h1_s"],
             "h2s": inp["L1_ffn_h2_s"], "w3s": inp["L1_ffn_w3_s"]} for c in range(NCORES)]
    r4 = run_bass_kernel_spmd(nc4, maps, core_ids=list(range(NCORES))).results

    out = np.empty((B, T, DM), np.float32)
    for c in range(NCORES):
        b, half = c // 2, c % 2
        out[b, half * TOWN:(half + 1) * TOWN, :] = r4[c]["xout"].T
    return out



def make_fused_program():
    nc = bass.Bass("TRN2", target_bir_lowering=False)
    xin = nc.dram_tensor("xin", [T, DM], F32, kind="ExternalInput").ap()
    sbm = nc.dram_tensor("sbmask", [16], F32, kind="ExternalInput").ap()
    hm = nc.dram_tensor("hmask", [1], F32, kind="ExternalInput").ap()
    wsrc = {}
    for nm, shp in [("w_in", [PROJ, 1024]), ("w_out", [1024, 2048]),
                    ("wqkv", [3072, 1024]), ("wproj", [1024, 1024]),
                    ("w1", [4096, 1024]), ("w2", [4096, 1024]), ("h1", [4096, 4096]),
                    ("h2", [4096, 4096]), ("w3", [1024, 4096])]:
        wsrc[nm] = nc.dram_tensor(nm, shp, F32, kind="ExternalInput").ap()
    bs = {}
    for nm, shp in [("in_proj_b", [PROJ]), ("conv_w", [2304, 4]), ("dt_bias", [32]),
                    ("A_log", [32]), ("L", [32, 15]), ("D", [32]), ("norm_w", [2048]),
                    ("out_proj_b", [1024])]:
        bs[nm] = nc.dram_tensor(nm, shp, F32, kind="ExternalInput").ap()
    k0 = nc.dram_tensor("k0", [2048, 4096], U8, kind="ExternalInput").ap()
    k1 = nc.dram_tensor("k1", [4096, 4096], U8, kind="ExternalInput").ap()
    k2 = nc.dram_tensor("k2", [4096, 1024], U8, kind="ExternalInput").ap()
    s0 = nc.dram_tensor("s0", [1], F32, kind="ExternalInput").ap()
    s1 = nc.dram_tensor("s1", [1], F32, kind="ExternalInput").ap()
    s2 = nc.dram_tensor("s2", [1], F32, kind="ExternalInput").ap()
    bproj = nc.dram_tensor("bproj", [1024], F32, kind="ExternalInput").ap()
    scales = {}
    for nm, n in [("w1s", 4096), ("w2s", 4096), ("h1s", 4096), ("h2s", 4096), ("w3s", 1024)]:
        scales[nm] = nc.dram_tensor(nm, [n], F32, kind="ExternalInput").ap()
    nats = {}
    scrt = {}
    for nm in wsrc:
        shp = wsrc[nm].shape
        nats[nm] = nc.dram_tensor(nm + "N", list(shp), BF16, kind="Internal").ap()
        scrt[nm] = nc.dram_tensor(nm + "T", [shp[1], shp[0]], BF16, kind="Internal").ap()
    x1 = nc.dram_tensor("x1i", [DM, TOWN], F32, kind="Internal").ap()
    x2 = nc.dram_tensor("x2i", [DM, TOWN], F32, kind="Internal").ap()
    x3 = nc.dram_tensor("x3i", [DM, TOWN], F32, kind="Internal").ap()
    xout = nc.dram_tensor("xout", [DM, TOWN], F32, kind="ExternalOutput").ap()

    p = Prog(nc)
    p.open_block()
    stgW = p.sb("wcstg", [128, 2 * 2048], BF16)
    jobsA = [(wsrc["w_in"], nats["w_in"], PROJ, 1024, "w_in"),
             (wsrc["w_out"], nats["w_out"], 1024, 2048, "w_out")]
    jobsB = [(wsrc["wqkv"], nats["wqkv"], 3072, 1024, "wqkv"),
             (wsrc["wproj"], nats["wproj"], 1024, 1024, "wproj"),
             (wsrc["w1"], nats["w1"], 4096, 1024, "w1"),
             (wsrc["w2"], nats["w2"], 4096, 1024, "w2"),
             (wsrc["w3"], nats["w3"], 1024, 4096, "w3")]
    jobsC = [(wsrc["h1"], nats["h1"], 4096, 4096, "h1"),
             (wsrc["h2"], nats["h2"], 4096, 4096, "h2")]
    mA = build_wcast(p, jobsA, stgW)
    tA = build_wtrans(p, [(nats["w_in"], scrt["w_in"], PROJ, 1024, "w_in", mA["w_out"]),
                          (nats["w_out"], scrt["w_out"], 1024, 2048, "w_out", mA["w_out"])], stgW)

    p.push_frame()
    build_b1(p, xin, scrt["w_in"], scrt["w_out"], bs, x1, sbm, hm, tseq=T,
             gate=tA["w_out"])
    barrier(p)
    p.pop_frame()

    mB = build_wcast(p, jobsB, stgW, gate=tA["w_out"])
    tB = build_wtrans(p, [(nats[nm], scrt[nm], wsrc[nm].shape[0], wsrc[nm].shape[1], nm, mB["w3"])
                          for nm in ["wqkv", "wproj", "w1", "w2", "w3"]], stgW)

    p.push_frame()
    build_b2(p, x1, k0, k1, k2, s0, s1, s2, x2)
    barrier(p)
    p.pop_frame()

    mC = build_wcast(p, jobsC, stgW, gate=tB["w3"])
    tC = build_wtrans(p, [(nats[nm], scrt[nm], 4096, 4096, nm, mC["h2"]) for nm in ["h1", "h2"]],
                      stgW)

    p.push_frame()
    build_b3(p, 0, x2, scrt["wqkv"], scrt["wproj"], bproj, x3, gate=tB["wproj"])
    barrier(p)
    p.pop_frame()

    p.push_frame()
    wT = {"w1": scrt["w1"], "w2": scrt["w2"], "h1": scrt["h1"], "h2": scrt["h2"],
          "w3": scrt["w3"]}
    build_b4(p, x3, wT, scales, xout, gate=tB["w3"], gate_bc=tC["h2"])
    p.pop_frame()
    finish(p)
    p.close()
    return nc


def kernel(**inp):
    x = np.ascontiguousarray(inp["L0_attn_hidden_states"], dtype=np.float32)
    zeros = np.zeros((TOWN, DM), np.float32)
    sb1 = np.ones(16, np.float32)
    sb0 = np.ones(16, np.float32)
    sb0[0:T // 2 // 512] = 0.0
    one = np.ones(1, np.float32)
    zero = np.zeros(1, np.float32)
    nc = _get_prog("fused", make_fused_program)
    k0 = np.ascontiguousarray(inp["L0_ffn_k0"]).view(np.uint8)
    k1 = np.ascontiguousarray(inp["L0_ffn_k1"]).view(np.uint8)
    k2 = np.ascontiguousarray(inp["L0_ffn_k2"]).view(np.uint8)
    maps = []
    for c in range(NCORES):
        b, half = c // 2, c % 2
        xin = x[b] if half == 1 else np.concatenate([zeros, x[b, :TOWN]], 0)
        maps.append({
            "xin": np.ascontiguousarray(xin),
            "sbmask": sb1 if half == 1 else sb0,
            "hmask": one if half == 1 else zero,
            "w_in": inp["L0_attn_in_proj_w"], "w_out": inp["L0_attn_out_proj_w"],
            "in_proj_b": inp["L0_attn_in_proj_b"], "conv_w": inp["L0_attn_conv_w"],
            "dt_bias": inp["L0_attn_dt_bias"], "A_log": inp["L0_attn_A_log"],
            "L": inp["L0_attn_L"], "D": inp["L0_attn_D"],
            "norm_w": inp["L0_attn_norm_w"], "out_proj_b": inp["L0_attn_out_proj_b"],
            "k0": k0, "k1": k1, "k2": k2,
            "s0": np.asarray(inp["L0_ffn_s0"], np.float32).reshape(1),
            "s1": np.asarray(inp["L0_ffn_s1"], np.float32).reshape(1),
            "s2": np.asarray(inp["L0_ffn_s2"], np.float32).reshape(1),
            "wqkv": inp["L1_attn_W_qkv"], "wproj": inp["L1_attn_W_proj"],
            "bproj": inp["L1_attn_b_proj"],
            "w1": inp["L1_ffn_w1_w"], "w2": inp["L1_ffn_w2_w"], "h1": inp["L1_ffn_h1_w"],
            "h2": inp["L1_ffn_h2_w"], "w3": inp["L1_ffn_w3_w"],
            "w1s": inp["L1_ffn_w1_s"], "w2s": inp["L1_ffn_w2_s"], "h1s": inp["L1_ffn_h1_s"],
            "h2s": inp["L1_ffn_h2_s"], "w3s": inp["L1_ffn_w3_s"],
        })
    res = run_bass_kernel_spmd(nc, maps, core_ids=list(range(NCORES))).results
    out = np.empty((B, T, DM), np.float32)
    for c in range(NCORES):
        b, half = c // 2, c % 2
        out[b, half * TOWN:(half + 1) * TOWN, :] = res[c]["xout"].T
    return out
